# version 01 (final)
# baseline (speedup 1.0000x reference)
import numpy as np
import ml_dtypes
import concourse.bass as bass
import concourse.mybir as mybir
from concourse.bass_utils import run_bass_kernel_spmd

F32 = mybir.dt.float32
BF16 = mybir.dt.bfloat16
AF = mybir.ActivationFunctionType
ALU = mybir.AluOpType
AX = mybir.AxisListType
BFNP = ml_dtypes.bfloat16
NCORE = 8
NEGBIG = np.float32(-3.0e38)
ENGINES = ("sync", "tensor", "vector", "scalar", "gpsimd")


class Prog:
    def __init__(self, nc):
        self.nc = nc
        self.ops = []
        self.sems = {e: nc.alloc_semaphore("s_" + e) for e in ENGINES}
        self.counts = {e: 0 for e in ENGINES}

    def named(self, name):
        if name not in self.sems:
            self.sems[name] = self.nc.alloc_semaphore("s_" + name)
            self.counts[name] = 0
        return self.sems[name]

    def val(self, eng):
        return self.counts[eng]

    def constap(self, v):
        if not hasattr(self, "_consts"):
            self._consts = {}
        if v not in self._consts:
            i = len(self._consts)
            t = self.nc.inline_tensor(np.full((128, 1), v, np.float32), f"cst{i}")
            sb = self.nc.alloc_sbuf_tensor(f"cstsb{i}", [128, 1], F32).ap()
            dv = self.dma("sync", sb, t.ap(), sem="cst")
            self._consts[v] = (sb, ("cst", dv))
        return self._consts[v]

    def op(self, engine, fn, waits=()):
        wl = [(e, v) for (e, v) in waits if v > 0]
        self.counts[engine] += 1
        self.ops.append((engine, wl, fn, (engine, 1)))
        return self.counts[engine]

    def dma(self, queue, out, in_, waits=(), sem=None, **kw):
        wl = [(e, v) for (e, v) in waits if v > 0]
        name = sem or queue
        self.named(name)
        self.counts[name] += 16
        self.ops.append((queue, wl,
                         lambda e: e.dma_start(out=out, in_=in_, **kw),
                         (name, 16)))
        return self.counts[name]

    def finalize(self):
        nc = self.nc
        used = [e for e in ENGINES if any(o[0] == e for o in self.ops)]
        cst_total = self.counts.get("cst", 0)
        myops = [(en, [(e2, cst_total if e2 == "cst" else v) for (e2, v) in wl], fn, inc)
                 for (en, wl, fn, inc) in self.ops]
        sems = self.sems

        def mk(name):
            def body(eng):
                for (en, wl, fn, inc) in myops:
                    if en != name:
                        continue
                    for (e2, v) in wl:
                        eng.wait_ge(sems[e2], v)
                    fn(eng).then_inc(sems[inc[0]], inc[1])
            return body

        with nc.Block() as block:
            for name in used:
                getattr(block, name)(mk(name))


def _mm(ps, lhsT, rhs, start, stop):
    return lambda e: e.matmul(ps, lhsT, rhs, start=start, stop=stop)


def _tr(ps, in_, ident):
    return lambda e: e.transpose(ps, in_, ident)


def _cp(out, in_):
    return lambda e: e.tensor_copy(out, in_)


def _act(out, in_, func, bias=0.0, scale=1.0):
    return lambda e: e.activation(out, in_, func, bias=bias, scale=scale)


def _ts(out, in0, s1, op0, s2=None, op1=None):
    if s2 is None:
        return lambda e: e.tensor_scalar(out, in0, s1, None, op0)
    return lambda e: e.tensor_scalar(out, in0, s1, s2, op0, op1)


def _tt(out, in0, in1, op):
    return lambda e: e.tensor_tensor(out, in0, in1, op)


def _stt(out, in0, s, in1, op0, op1):
    return lambda e: e.scalar_tensor_tensor(out, in0, s, in1, op0, op1)


def _red(out, in_, op):
    return lambda e: e.tensor_reduce(out, in_, AX.X, op)


def _memset(ap, v):
    return lambda e: e.memset(ap, v)


TOK = 2176
TOKS = 2051
OWN = 2048
NTI = TOK // 128


def build_a1():
    nc = bass.Bass("TRN2", target_bir_lowering=False, dynamic_dma_scratch_size=16384)
    xin = nc.dram_tensor("xin", [TOK, 4096], F32, kind="ExternalInput").ap()
    w_in = nc.dram_tensor("w_in", [4096, 16704], F32, kind="ExternalInput").ap()
    convw = nc.dram_tensor("convw", [8448, 4], F32, kind="ExternalInput").ap()
    z_out = nc.dram_tensor("z_out", [8192, OWN], BF16, kind="ExternalOutput").ap()
    xbc_out = nc.dram_tensor("xbc_out", [8448, OWN], BF16, kind="ExternalOutput").ap()
    dt_out = nc.dram_tensor("dt_out", [64, OWN], F32, kind="ExternalOutput").ap()

    ident = nc.inline_tensor(np.eye(128, dtype=np.float32), "ident").ap()

    p = Prog(nc)
    A = nc.alloc_sbuf_tensor

    sb_id = A("sb_id", [128, 128], F32).ap()
    stage = A("stage", [128, 2048], F32).ap()
    xT = [A(f"xT{k}", [128, TOKS], BF16).ap() for k in range(32)]
    wst = A("wst", [128, 2048], F32).ap()
    wbf = [A(f"wbf{i}", [128, 4096], BF16).ap() for i in range(2)]
    cwsb = A("cwsb", [128, 264], F32).ap()
    convbuf = [A(f"convbuf{i}", [128, TOKS], F32).ap() for i in range(2)]
    convtmp = A("convtmp", [128, OWN], F32).ap()
    outsl = [A(f"outsl{i}", [128, 2048], BF16).ap() for i in range(2)]
    dtsl = [A(f"dtsl{i}", [64, 512], F32).ap() for i in range(2)]

    ps_tr = [nc.alloc_psum_tensor(f"ps_tr{i}", [128, 128], F32).ap() for i in range(2)]
    TCW = [512, 512, 512, 512, 3]
    TCO = [0, 512, 1024, 1536, 2048]
    ps = [nc.alloc_psum_tensor(f"psA{t}", [128, TCW[t]], F32).ap() for t in range(5)]

    p.dma("sync", sb_id, ident, sem="cst")
    cw_ap = bass.AP(convw.tensor, 0, [[4, 128], [512, 66], [1, 4]])
    p.dma("sync", cwsb, cw_ap, sem="cst")
    ntr = 0
    trv = 0
    cpv_hist = []
    stage_done = 0
    for i in range(NTI):
        cw = 128 if i < 16 else 3
        for half in range(2):
            w_ld = [("tensor", stage_done)] if (i, half) != (0, 0) else []
            lxv = p.dma("sync", stage, xin[i * 128:(i + 1) * 128, half * 2048:(half + 1) * 2048],
                        waits=w_ld, sem="stx")
            for kh in range(16):
                k = half * 16 + kh
                slot = ntr % 2
                w = [("stx", lxv)]
                if ntr == 0:
                    w.append(("cst", 1))
                if ntr >= 2:
                    w.append(("vector", cpv_hist[ntr - 2]))
                trv = p.op("tensor", _tr(ps_tr[slot], stage[:, kh * 128:(kh + 1) * 128], sb_id), waits=w)
                cpv = p.op("vector", _cp(xT[k][:, i * 128:i * 128 + cw], ps_tr[slot][:, 0:cw]),
                           waits=[("tensor", trv)])
                cpv_hist.append(cpv)
                ntr += 1
            stage_done = trv
    STAGE_CP = cpv_hist[-1]

    nout = 0
    last_silu = [0]
    dt_dma = []
    evc = [("vector", 0)] * 5
    wfree = [("tensor", 0), ("tensor", 0)]
    out_dma = []
    conv_dma = [0, 0]
    castv = 0
    pev = 0
    for m in range(131):
        mw = 128 if m < 130 else 64
        for half in range(2):
            w_ap = bass.AP(w_in.tensor, m * 128 + half * 16 * 128 * 16704,
                           [[16704, 128], [128 * 16704, 16], [1, mw]])
            lwv = p.dma("sync", wst[:, 0:16 * mw], w_ap, waits=[("vector", castv)], sem="w")
            castv = p.op("vector",
                         _cp(wbf[m % 2][:, half * 16 * mw:(half + 1) * 16 * mw], wst[:, 0:16 * mw]),
                         waits=[("w", lwv)] + ([wfree[m % 2]] if half == 0 else []))
        for tc in range(5):
            for k in range(32):
                waits = []
                if k == 0:
                    waits = [("vector", castv), evc[tc]]
                    if m == 0:
                        waits.append(("vector", STAGE_CP))
                pev = p.op("tensor",
                           _mm(ps[tc][0:mw, :], wbf[m % 2][:, k * mw:(k + 1) * mw],
                               xT[k][:, TCO[tc]:TCO[tc] + TCW[tc]], k == 0, k == 31),
                           waits=waits)
            if m < 64:
                slot = nout % 2
                w = [("tensor", pev)]
                if nout >= 2:
                    w.append(out_dma[nout - 2])
                ev = p.op("scalar", _act(outsl[slot][:, 0:TCW[tc]], ps[tc], AF.Silu), waits=w)
                evc[tc] = ("scalar", ev)
                src, dst, wdt = _own_cols(tc)
                dv = p.dma("gpsimd", z_out[m * 128:(m + 1) * 128, dst:dst + wdt],
                           outsl[slot][:, src:src + wdt], waits=[("scalar", ev)],
                           sem=f"ob{slot}")
                out_dma.append((f"ob{slot}", dv))
                nout += 1
            elif m < 130:
                cslot = m % 2
                w = [("tensor", pev)]
                if tc == 0 and m >= 66:
                    w.append(("scalar", conv_dma[cslot]))
                ev = p.op("vector", _cp(convbuf[cslot][:, TCO[tc]:TCO[tc] + TCW[tc]], ps[tc]), waits=w)
                evc[tc] = ("vector", ev)
                if tc == 4:
                    mc = m - 64
                    wcol = lambda kk: cwsb[:, mc * 4 + kk:mc * 4 + kk + 1]
                    vv = p.op("vector", _ts(convtmp, convbuf[cslot][:, 0:OWN], wcol(0), ALU.mult),
                              waits=[("vector", ev), ("scalar", last_silu[0]), ("cst", 1)])
                    for kk in range(1, 4):
                        vv = p.op("vector", _stt(convtmp, convbuf[cslot][:, kk:OWN + kk],
                                                 wcol(kk), convtmp, ALU.mult, ALU.add),
                                  waits=[("vector", vv)])
                    slot = nout % 2
                    w2 = [("vector", vv)]
                    if nout >= 2:
                        w2.append(out_dma[nout - 2])
                    sv = p.op("scalar", _act(outsl[slot][:, 0:OWN], convtmp, AF.Silu), waits=w2)
                    conv_dma[cslot] = sv
                    last_silu[0] = sv
                    dv = p.dma("gpsimd", xbc_out[mc * 128:(mc + 1) * 128, :], outsl[slot][:, 0:OWN],
                               waits=[("scalar", sv)], sem=f"ob{slot}")
                    out_dma.append((f"ob{slot}", dv))
                    nout += 1
            else:
                slot = tc % 2
                w = [("tensor", pev)]
                if tc >= 2:
                    w.append(dt_dma[tc - 2])
                ev = p.op("vector", _cp(dtsl[slot][:, 0:TCW[tc]], ps[tc][0:64, :]), waits=w)
                evc[tc] = ("vector", ev)
                src_c, dst_c, wdt = _own_cols(tc)
                dv = p.dma("gpsimd", dt_out[:, dst_c:dst_c + wdt],
                           dtsl[slot][:, src_c:src_c + wdt], waits=[("vector", ev)],
                           sem=f"od{slot}")
                dt_dma.append((f"od{slot}", dv))
        wfree[m % 2] = ("tensor", pev)
    p.op("gpsimd", lambda e: e.nop(),
         waits=[(s, p.val(s)) for s in ("ob0", "ob1", "od0", "od1")])
    p.finalize()
    return nc


def _own_cols(tc):
    if tc == 0:
        return 3, 0, 509
    if tc < 4:
        return 0, tc * 512 - 3, 512
    return 0, 2045, 3


def run_a1(x_flat, W_in, conv_w):
    nc = build_a1()
    in_maps = []
    for c in range(NCORE):
        start = c * OWN
        xin = np.zeros((TOK, 4096), dtype=np.float32)
        lo = max(start - 3, (start // 8192) * 8192)
        npre = start - lo
        xin[3 - npre:3 + OWN] = x_flat[lo:start + OWN]
        in_maps.append({"xin": xin, "w_in": W_in, "convw": conv_w})
    res = run_bass_kernel_spmd(nc, in_maps, core_ids=list(range(NCORE)))
    z_s = np.concatenate([res.results[c]["z_out"] for c in range(NCORE)], axis=1)
    xbc_s = np.concatenate([res.results[c]["xbc_out"] for c in range(NCORE)], axis=1)
    dt_s = np.concatenate([res.results[c]["dt_out"] for c in range(NCORE)], axis=1)
    return z_s, xbc_s, dt_s


def build_a2(NH=16, NCk=64):
    nc = bass.Bass("TRN2", target_bir_lowering=False)
    xs = nc.dram_tensor("xs", [2048, 8192], BF16, kind="ExternalInput").ap()
    bc = nc.dram_tensor("bc", [256, 8192], BF16, kind="ExternalInput").ap()
    dtr = nc.dram_tensor("dtr", [16, 8192], F32, kind="ExternalInput").ap()
    dbias = nc.dram_tensor("dbias", [16, 1], F32, kind="ExternalInput").ap()
    alog = nc.dram_tensor("alog", [16, 1], F32, kind="ExternalInput").ap()
    dpar = nc.dram_tensor("dpar", [16, 1], F32, kind="ExternalInput").ap()
    y_out = nc.dram_tensor("y_out", [2048, 8192], BF16, kind="ExternalOutput").ap()

    identf = nc.inline_tensor(np.eye(128, dtype=np.float32), "identf").ap()
    identb = nc.inline_tensor(np.eye(128, dtype=BFNP), "identb").ap()
    e16_np = np.zeros((16, 16 * 128), dtype=np.float32)
    for h in range(16):
        e16_np[h, h * 128:(h + 1) * 128] = 1.0
    e16t = nc.inline_tensor(e16_np, "e16").ap()
    negtri_np = np.where(np.tril(np.ones((128, 128)), 0).T.astype(bool), 0.0,
                         np.float32(-3e38)).astype(np.float32)
    negtrit = nc.inline_tensor(negtri_np, "negtri").ap()
    ones16_np = np.ones((16, 128), dtype=np.float32)
    ones16t = nc.inline_tensor(ones16_np, "ones16").ap()
    i16_np = np.eye(16, dtype=np.float32)
    i16t = nc.inline_tensor(i16_np, "i16").ap()

    p = Prog(nc)
    A = nc.alloc_sbuf_tensor
    sb_idf = A("sb_idf", [128, 128], F32).ap()
    sb_idb = A("sb_idb", [128, 128], BF16).ap()
    sb_e16 = A("sb_e16", [16, 2048], F32).ap()
    sb_negtri = A("sb_negtri", [128, 128], F32).ap()
    sb_ones16 = A("sb_ones16", [16, 128], F32).ap()
    sb_i16 = A("sb_i16", [16, 16], F32).ap()
    sb_dbias = A("sb_dbias", [16, 1], F32).ap()
    sb_alog = A("sb_alog", [16, 1], F32).ap()
    sb_dpar = A("sb_dpar", [16, 1], F32).ap()
    sb_negea = A("sb_negea", [16, 1], F32).ap()
    sb_ddiag = A("sb_ddiag", [16, 16], F32).ap()
    sb_dbc = A("sb_dbc", [128, 16], F32).ap()
    R0 = A("R0", [16, 8192], F32).ap()
    R1 = A("R1", [16, 8192], F32).ap()
    C_bf = A("C_bf", [128, 8192], BF16).ap()
    CBT_bf = A("CBT_bf", [128, 8192], BF16).ap()
    Btm = A("Btm", [128, 8192], BF16).ap()
    xh = [A(f"xh{i}", [128, 8192], BF16).ap() for i in range(2)]
    bstage = [A(f"bstage{i}", [128, 128], BF16).ap() for i in range(2)]
    dttm = A("dttm", [128, 1024], F32).ap()
    Ltm = A("Ltm", [128, 1024], F32).ap()
    negLtm = A("negLtm", [128, 1024], F32).ap()
    abc = A("abc", [128, 1024], F32).ap()
    negLpb = A("negLpb", [128, 1024], F32).ap()
    cs16 = A("cs16", [16, 64], F32).ap()
    csx = A("csx", [16, 64], F32).ap()
    sb_t0 = A("sb_t0", [128, 128], F32).ap()
    sb_DT = A("sb_DT", [128, 128], F32).ap()
    sb_MT = A("sb_MT", [128, 128], BF16).ap()
    sb_wB = A("sb_wB", [128, 128], BF16).ap()
    sb_xtm = [A(f"sb_xtm{i}", [128, 128], BF16).ap() for i in range(2)]
    sb_dcol = A("sb_dcol", [128, 1], F32).ap()
    sb_ya = A("sb_ya", [128, 128], F32).ap()
    sb_yb = A("sb_yb", [128, 128], F32).ap()
    sb_ytm = A("sb_ytm", [128, 128], BF16).ap()
    S_f = [A(f"S_f{i}", [128, 128], F32).ap() for i in range(2)]
    Sprev = A("Sprev", [128, 128], BF16).ap()
    yst = [A(f"yst{i}", [128, 128], BF16).ap() for i in range(2)]

    ps_a = nc.alloc_psum_tensor("ps_a", [128, 128], F32).ap()
    ps_y = nc.alloc_psum_tensor("ps_y", [128, 128], F32).ap()
    ps_cs = nc.alloc_psum_tensor("ps_cs", [128, 128], F32).ap()
    ps_s = nc.alloc_psum_tensor("ps_s", [128, 128], F32).ap()
    ps_tf = nc.alloc_psum_tensor("ps_tf", [128, 128], F32).ap()
    ps_tb = nc.alloc_psum_tensor("ps_tb", [128, 128], BF16).ap()
    ps_tb2 = nc.alloc_psum_tensor("ps_tb2", [128, 128], BF16).ap()

    for (dst, src) in ((sb_idf, identf), (sb_idb, identb), (sb_e16, e16t),
                       (sb_negtri, negtrit), (sb_ones16, ones16t), (sb_i16, i16t),
                       (sb_dbias, dbias), (sb_alog, alog), (sb_dpar, dpar),
                       (R0, dtr)):
        p.dma("sync", dst, src, sem="cst")
    ld0 = p.dma("sync", C_bf, bc[128:256, :], sem="cst")
    v = p.op("vector", _ts(R1, R0, sb_dbias, ALU.add), waits=[("cst", ld0)])
    s1 = p.op("scalar", _act(R0, R1, AF.Abs), waits=[("vector", v)])
    s1 = p.op("scalar", _act(R0, R0, AF.Exp, scale=-1.0), waits=[("scalar", s1)])
    s1 = p.op("scalar", _act(R0, R0, AF.Ln, bias=1.0), waits=[("scalar", s1)])
    v = p.op("vector", _stt(R0, R1, 0.0, R0, ALU.max, ALU.add), waits=[("scalar", s1)])
    DTPOS_DONE = v
    s1 = p.op("scalar", _act(sb_negea, sb_alog, AF.Exp), waits=[("cst", ld0)])
    v2 = p.op("vector", _ts(sb_negea, sb_negea, -1.0, ALU.mult), waits=[("scalar", s1)])
    trs = 0
    evs = 0
    for c in range(64):
        trs = p.op("tensor", _tr(ps_tf[:, 0:16], R0[:, c * 128:(c + 1) * 128], sb_idf[0:16, 0:16]),
                   waits=[("vector", DTPOS_DONE), ("vector", evs)])
        evs = p.op("vector", _cp(dttm[:, c * 16:(c + 1) * 16], ps_tf[:, 0:16]),
                   waits=[("tensor", trs)])
    v = p.op("vector", _ts(R1, R0, sb_negea, ALU.mult), waits=[("vector", max(evs, v2))])
    src_, dst_ = R1, R0
    for j in range(13):
        s = 1 << j
        v1 = p.op("vector", _cp(dst_[:, 0:s], src_[:, 0:s]), waits=[("vector", v)])
        v = p.op("vector", _tt(dst_[:, s:8192], src_[:, s:8192], src_[:, 0:8192 - s], ALU.add),
                 waits=[("vector", v1)])
        src_, dst_ = dst_, src_
    Lg = src_
    assert Lg is R0
    LG_DONE = v
    for c in range(64):
        trs = p.op("tensor", _tr(ps_tf[:, 0:16], Lg[:, c * 128:(c + 1) * 128], sb_idf[0:16, 0:16]),
                   waits=[("vector", max(LG_DONE, evs))])
        evs = p.op("vector", _cp(Ltm[:, c * 16:(c + 1) * 16], ps_tf[:, 0:16]),
                   waits=[("tensor", trs)])
        evs = p.op("vector", _ts(negLtm[:, c * 16:(c + 1) * 16], ps_tf[:, 0:16], -1.0, ALU.mult),
                   waits=[("tensor", trs)])
    lge = Lg[:, 127::128]
    v1 = p.op("vector", _cp(cs16[:, 0:1], lge[:, 0:1]), waits=[("vector", LG_DONE)])
    v1 = p.op("vector", _tt(cs16[:, 1:64], lge[:, 1:64], lge[:, 0:63], ALU.subtract),
              waits=[("vector", v1)])
    sc1 = p.op("scalar", _act(cs16, cs16, AF.Exp), waits=[("vector", v1)])
    v1 = p.op("vector", lambda e: e.memset(csx[:, 0:1], 0.0), waits=[("vector", v1)])
    v1 = p.op("vector", _ts(csx[:, 1:64], lge[:, 0:63], -1.0, ALU.mult), waits=[("vector", v1)])
    v1 = p.op("vector", _ts(sb_ddiag, sb_i16, sb_dpar, ALU.mult), waits=[("cst", ld0)])
    t1 = p.op("tensor", _mm(ps_a[:, 0:16], sb_ones16, sb_ddiag, True, True),
              waits=[("vector", v1)])
    v1 = p.op("vector", _cp(sb_dbc, ps_a[:, 0:16]), waits=[("tensor", t1)])
    for h in range(16):
        t1 = p.op("tensor", _mm(ps_a[:, 0:64], sb_e16[:, h * 128:(h + 1) * 128],
                                cs16, True, True),
                  waits=[("scalar", sc1), ("vector", p.val("vector"))])
        v1 = p.op("vector", _cp(abc[:, h * 64:(h + 1) * 64], ps_a[:, 0:64]),
                  waits=[("tensor", t1)])
        t1 = p.op("tensor", _mm(ps_a[:, 0:64], sb_e16[:, h * 128:(h + 1) * 128],
                                csx, True, True),
                  waits=[("vector", v1)])
        v1 = p.op("vector", _cp(negLpb[:, h * 64:(h + 1) * 64], ps_a[:, 0:64]),
                  waits=[("tensor", t1)])
    PRE_DONE_V = v1

    bst_free = [("vector", 0), ("vector", 0)]
    for c in range(NCk):
        lb = p.dma("sync", bstage[c % 2], bc[0:128, c * 128:(c + 1) * 128],
                   waits=[bst_free[c % 2]], sem=f"bs{c % 2}")
        t1 = p.op("tensor", _mm(ps_a, bstage[c % 2], C_bf[:, c * 128:(c + 1) * 128], True, True),
                  waits=[(f"bs{c % 2}", lb), ("vector", p.val("vector"))])
        v1 = p.op("vector", _cp(CBT_bf[:, c * 128:(c + 1) * 128], ps_a), waits=[("tensor", t1)])
        t2 = p.op("tensor", _tr(ps_tb, bstage[c % 2], sb_idb), waits=[("vector", v1)])
        v1 = p.op("vector", _cp(Btm[:, c * 128:(c + 1) * 128], ps_tb), waits=[("tensor", t2)])
        bst_free[c % 2] = ("vector", v1)
    PREP2 = v1

    xh_ld = [0, 0]
    xh_free = [("tensor", 0), ("tensor", 0)]
    xh_ld[0] = p.dma("sync", xh[0], xs[0:128, :], sem="xh0")
    xh_ld[1] = p.dma("sync", xh[1], xs[128:256, :], sem="xh1")
    ydma = []
    f_psy = ("vector", 0)
    f_pscs = ("vector", 0)
    f_pss = ("vector", 0)
    f_psa = ("vector", PREP2)
    f_pstb = ("vector", PREP2)
    f_pstb2 = ("vector", 0)
    xtm_free = [("tensor", 0), ("tensor", 0)]
    mt_free = ("tensor", 0)
    wb_free = ("tensor", 0)
    t0_free = ("scalar", 0)
    dt_free = ("vector", 0)
    ya_free = ("vector", 0)
    yb_free = ("vector", 0)
    ytm_free = ("tensor", 0)
    sprev_rd = ("tensor", 0)
    sprev_wr = ("vector", 0)
    last_vs = 0
    last_vsp = 0
    dcol_free = ("vector", 0)
    for h in range(NH):
        if h >= 2:
            xh_ld[h % 2] = p.dma("sync", xh[h % 2], xs[h * 128:(h + 1) * 128, :],
                                 waits=[xh_free[h % 2]], sem=f"xh{h % 2}")
        last_tx = 0
        for c in range(NCk):
            xsl = h % 2
            t_x = p.op("tensor", _tr(ps_tb2, xh[xsl][:, c * 128:(c + 1) * 128], sb_idb),
                       waits=[(f"xh{xsl}", xh_ld[xsl]), f_pstb2, ("vector", PRE_DONE_V)])
            last_tx = t_x
            xslot = c % 2
            v_x = p.op("vector", _cp(sb_xtm[xslot], ps_tb2), waits=[("tensor", t_x), xtm_free[xslot]])
            f_pstb2 = ("vector", v_x)
            t_l = p.op("tensor", _mm(ps_a, sb_e16[:, h * 128:(h + 1) * 128],
                                     Lg[:, c * 128:(c + 1) * 128], True, True),
                       waits=[f_psa])
            v_t0 = p.op("vector", _tt(sb_t0, ps_a, sb_negtri, ALU.add),
                        waits=[("tensor", t_l), t0_free])
            f_psa = ("vector", v_t0)
            s_dt = p.op("scalar", _act(sb_DT, sb_t0, AF.Exp,
                                       bias=negLtm[:, c * 16 + h:c * 16 + h + 1]),
                        waits=[("vector", v_t0), dt_free])
            t0_free = ("scalar", s_dt)
            v_mt = p.op("vector", _stt(sb_MT, sb_DT, dttm[:, c * 16 + h:c * 16 + h + 1],
                                       CBT_bf[:, c * 128:(c + 1) * 128], ALU.mult, ALU.mult),
                        waits=[("scalar", s_dt), mt_free])
            v_wb = p.op("vector", _ts(sb_wB, Btm[:, c * 128:(c + 1) * 128],
                                      dttm[:, c * 16 + h:c * 16 + h + 1],
                                      ALU.mult, sb_DT[:, 127:128], ALU.mult),
                        waits=[("scalar", s_dt), wb_free])
            dt_free = ("vector", v_wb)
            t_y = p.op("tensor", _mm(ps_y, sb_MT, sb_xtm[xslot], True, True),
                       waits=[("vector", max(v_mt, v_x)), f_psy])
            mt_free = ("tensor", t_y)
            if c > 0:
                t_cs = p.op("tensor", _mm(ps_cs, C_bf[:, c * 128:(c + 1) * 128], Sprev, True, True),
                            waits=[sprev_wr, f_pscs])
                sprev_rd = ("tensor", t_cs)
                s_dc = p.op("scalar", _act(sb_dcol, Ltm[:, c * 16 + h:c * 16 + h + 1], AF.Exp,
                                           bias=negLpb[:, h * 64 + c:h * 64 + c + 1]),
                            waits=[dcol_free])
            t_s = p.op("tensor", _mm(ps_s, sb_wB, sb_xtm[xslot], True, True),
                       waits=[("vector", v_wb), f_pss])
            wb_free = ("tensor", t_s)
            xtm_free[xslot] = ("tensor", t_s)
            if c == 0:
                v_s = p.op("vector", _cp(S_f[0], ps_s),
                           waits=[("tensor", t_s), ("vector", last_vsp)])
            else:
                v_s = p.op("vector", _stt(S_f[c % 2], S_f[(c - 1) % 2],
                                          abc[:, h * 64 + c:h * 64 + c + 1], ps_s,
                                          ALU.mult, ALU.add),
                           waits=[("tensor", t_s), ("vector", last_vsp)])
            last_vs = v_s
            f_pss = ("vector", v_s)
            v_sp = p.op("vector", _cp(Sprev, S_f[c % 2]),
                        waits=[("vector", v_s), sprev_rd])
            last_vsp = v_sp
            sprev_wr = ("vector", v_sp)
            if c > 0:
                v_ya = p.op("vector", _ts(sb_ya, ps_cs, sb_dcol, ALU.mult),
                            waits=[("tensor", t_cs), ("scalar", s_dc), ya_free])
                f_pscs = ("vector", v_ya)
                dcol_free = ("vector", v_ya)
            v_yb = p.op("vector", _stt(sb_yb, sb_xtm[xslot], sb_dbc[:, h:h + 1], ps_y,
                                       ALU.mult, ALU.add),
                        waits=[("tensor", t_y), yb_free])
            f_psy = ("vector", v_yb)
            if c > 0:
                v_yf = p.op("vector", _tt(sb_ytm, sb_ya, sb_yb, ALU.add),
                            waits=[("vector", v_yb), ytm_free])
                ya_free = ("vector", v_yf)
            else:
                v_yf = p.op("vector", _cp(sb_ytm, sb_yb), waits=[("vector", v_yb), ytm_free])
            yb_free = ("vector", v_yf)
            t_yt = p.op("tensor", _tr(ps_tb, sb_ytm, sb_idb), waits=[("vector", v_yf), f_pstb])
            ytm_free = ("tensor", t_yt)
            yslot = (h * 64 + c) % 2
            w = [("tensor", t_yt)]
            if len(ydma) >= 2:
                w.append(ydma[-2])
            v_ye = p.op("vector", _cp(yst[yslot], ps_tb), waits=w)
            f_pstb = ("vector", v_ye)
            dv = p.dma("gpsimd", y_out[h * 128:(h + 1) * 128, c * 128:(c + 1) * 128],
                       yst[yslot], waits=[("vector", v_ye)], sem=f"yo{yslot}")
            ydma.append((f"yo{yslot}", dv))
        xh_free[h % 2] = ("tensor", last_tx)
    p.op("gpsimd", lambda e: e.nop(),
         waits=[(s, p.val(s)) for s in ("yo0", "yo1")])
    p.finalize()
    return nc


def run_a2(xbc_s, dt_s, dt_bias, A_log, D_param):
    nc = build_a2()
    in_maps = []
    for c in range(NCORE):
        b, hg = c // 4, c % 4
        cols = slice(b * 8192, (b + 1) * 8192)
        in_maps.append({
            "xs": np.ascontiguousarray(xbc_s[hg * 2048:(hg + 1) * 2048, cols]),
            "bc": np.ascontiguousarray(xbc_s[8192:8448, cols]),
            "dtr": np.ascontiguousarray(dt_s[hg * 16:(hg + 1) * 16, cols]),
            "dbias": np.ascontiguousarray(dt_bias[hg * 16:(hg + 1) * 16, None]),
            "alog": np.ascontiguousarray(A_log[hg * 16:(hg + 1) * 16, None]),
            "dpar": np.ascontiguousarray(D_param[hg * 16:(hg + 1) * 16, None]),
        })
    res = run_bass_kernel_spmd(nc, in_maps, core_ids=list(range(NCORE)))
    y_fm = np.zeros((8192, 16384), dtype=BFNP)
    for c in range(NCORE):
        b, hg = c // 4, c % 4
        y_fm[hg * 2048:(hg + 1) * 2048, b * 8192:(b + 1) * 8192] = res.results[c]["y_out"]
    return y_fm


def emit_wload(p, wtens, row_len, m_off, mw, KC, wst, wbf, wait_free, castv):
    part_kc = max(1, 2048 // mw)
    n_parts = (KC + part_kc - 1) // part_kc
    for part in range(n_parts):
        kc0 = part * part_kc
        kcn = min(part_kc, KC - kc0)
        w_ap = bass.AP(wtens, m_off + kc0 * 128 * row_len,
                       [[row_len, 128], [128 * row_len, kcn], [1, mw]])
        lwv = p.dma("sync", wst[:, 0:kcn * mw], w_ap, waits=[("vector", castv)], sem="w")
        castv = p.op("vector",
                     _cp(wbf[:, kc0 * mw:(kc0 + kcn) * mw], wst[:, 0:kcn * mw]),
                     waits=[("w", lwv)] + ([wait_free] if part == 0 else []))
    return castv


class LNctx:
    def __init__(self, nc, p, tag, TTW=512):
        A = nc.alloc_sbuf_tensor
        onesc = nc.inline_tensor(np.ones((128, 1), dtype=np.float32), f"onesc{tag}").ap()
        ones1 = nc.inline_tensor(np.ones((1, 128), dtype=np.float32), f"ones1{tag}").ap()
        self.sb_onesc = A(f"ln_onesc{tag}", [128, 1], F32).ap()
        self.sb_ones1 = A(f"ln_ones1{tag}", [1, 128], F32).ap()
        p.dma("sync", self.sb_onesc, onesc, sem="cst")
        self.ldv = p.dma("sync", self.sb_ones1, ones1, sem="cst")
        self.sq = [A(f"ln_sq{tag}{i}", [128, TTW], F32).ap() for i in range(2)]
        self.rows = A(f"ln_rows{tag}", [1, 4 * TTW], F32).ap()
        self.nm_bc = A(f"ln_nm{tag}", [128, TTW], F32).ap()
        self.rs_bc = A(f"ln_rs{tag}", [128, TTW], F32).ap()
        self.ps_sum = nc.alloc_psum_tensor(f"ln_psum{tag}", [1, TTW], F32).ap()
        self.ps_sq = nc.alloc_psum_tensor(f"ln_psq{tag}", [1, TTW], F32).ap()
        self.ps_bc = nc.alloc_psum_tensor(f"ln_pbc{tag}", [128, TTW], F32).ap()
        self.TTW = TTW
        self.sq_free = [("tensor", 0), ("tensor", 0)]
        self.f_sum = ("vector", 0)
        self.f_bc = ("vector", 0)


def emit_ln(p, ln, outT, n_oc, ready, write_fn, D=4096.0):
    TTW = ln.TTW
    sv = 0
    t = 0
    for oc in range(n_oc):
        t = p.op("tensor", _mm(ln.ps_sum, ln.sb_onesc, outT[oc], oc == 0, oc == n_oc - 1),
                 waits=[ready[oc], ln.f_sum, ("cst", ln.ldv)])
        sv = p.op("scalar", _act(ln.sq[oc % 2], outT[oc], AF.Square),
                  waits=[ready[oc], ln.sq_free[oc % 2]])
        t = p.op("tensor", _mm(ln.ps_sq, ln.sb_onesc, ln.sq[oc % 2], oc == 0, oc == n_oc - 1),
                 waits=[("scalar", sv)])
        ln.sq_free[oc % 2] = ("tensor", t)
    v = p.op("vector", _ts(ln.rows[:, 0:TTW], ln.ps_sum, 1.0 / D, ALU.mult),
             waits=[("tensor", t), ln.f_bc])
    s = p.op("scalar", _act(ln.rows[:, TTW:2 * TTW], ln.rows[:, 0:TTW], AF.Square),
             waits=[("vector", v)])
    v2 = p.op("vector", _stt(ln.rows[:, 2 * TTW:3 * TTW], ln.ps_sq, 1.0 / D,
                             ln.rows[:, TTW:2 * TTW], ALU.mult, ALU.subtract),
              waits=[("scalar", s)])
    c5, c5w = p.constap(1e-5)
    s = p.op("scalar", _act(ln.rows[:, 2 * TTW:3 * TTW], ln.rows[:, 2 * TTW:3 * TTW],
                            AF.Sqrt, bias=c5[0:1, :]), waits=[("vector", v2), c5w])
    v2 = p.op("vector", lambda e: e.reciprocal(ln.rows[:, 3 * TTW:4 * TTW],
                                               ln.rows[:, 2 * TTW:3 * TTW]),
              waits=[("scalar", s)])
    vneg = p.op("vector", _ts(ln.rows[:, 0:TTW], ln.rows[:, 0:TTW], -1.0, ALU.mult),
                waits=[("vector", v2)])
    t = p.op("tensor", _mm(ln.ps_bc, ln.sb_ones1, ln.rows[0:1, 0:TTW], True, True),
             waits=[("vector", vneg), ln.f_bc])
    vn = p.op("vector", _cp(ln.nm_bc, ln.ps_bc), waits=[("tensor", t)])
    t = p.op("tensor", _mm(ln.ps_bc, ln.sb_ones1, ln.rows[0:1, 3 * TTW:4 * TTW], True, True),
             waits=[("vector", vn)])
    vr = p.op("vector", _cp(ln.rs_bc, ln.ps_bc), waits=[("tensor", t)])
    ln.f_bc = ("vector", vr)
    last = vr
    ln.f_sum = ("vector", vr)
    wcounts = []
    for oc in range(n_oc):
        sl = oc % 2
        v3 = p.op("vector", _tt(ln.sq[sl], outT[oc], ln.nm_bc, ALU.add),
                  waits=[("vector", max(vn, last)), ln.sq_free[sl]])
        v3 = p.op("vector", _tt(outT[oc], ln.sq[sl], ln.rs_bc, ALU.mult),
                  waits=[("vector", v3)])
        ln.sq_free[sl] = ("vector", v3)
        wcounts.append(write_fn(oc, ("vector", v3)))
    return wcounts


def build_a3():
    nc = bass.Bass("TRN2", target_bir_lowering=False)
    yin = nc.dram_tensor("yin", [8192, 2048], BF16, kind="ExternalInput").ap()
    szin = nc.dram_tensor("szin", [8192, 2048], BF16, kind="ExternalInput").ap()
    xfm = nc.dram_tensor("xfm", [4096, 2048], F32, kind="ExternalInput").ap()
    wout = nc.dram_tensor("wout", [8192, 4096], F32, kind="ExternalInput").ap()
    nrmw = nc.dram_tensor("nrmw", [128, 64], F32, kind="ExternalInput").ap()
    x1o = nc.dram_tensor("x1o", [4096, 2048], F32, kind="ExternalOutput").ap()

    onesb = nc.inline_tensor(np.ones((128, 1), dtype=BFNP), "onesb").ap()
    ones1f = nc.inline_tensor(np.ones((1, 128), dtype=np.float32), "ones1f").ap()

    p = Prog(nc)
    A = nc.alloc_sbuf_tensor
    ln = LNctx(nc, p, "a3")
    sb_onesb = A("sb_onesb", [128, 1], BF16).ap()
    sb_ones1f = A("sb_ones1f", [1, 128], F32).ap()
    nw = A("nw", [128, 64], F32).ap()
    gated = [A(f"ga{k}", [128, 512], BF16).ap() for k in range(64)]
    outT = [A(f"oT{m}", [128, 512], F32).ap() for m in range(32)]
    ystage = [A(f"yst{i}", [128, 512], BF16).ap() for i in range(2)]
    zstage = [A(f"zst{i}", [128, 512], BF16).ap() for i in range(2)]
    sqst = [A(f"sqst{i}", [128, 512], BF16).ap() for i in range(2)]
    rstdbc = A("rstdbc", [128, 512], F32).ap()
    rrow = A("rrow", [1, 512], F32).ap()
    dbg_row_sb = A("dbg_row_sb", [1, 512], F32).ap()
    wst = A("wstA3", [128, 2048], F32).ap()
    wbf = [A(f"wbfA3{i}", [128, 8192], BF16).ap() for i in range(2)]
    resst = [A(f"resst{i}", [128, 512], F32).ap() for i in range(2)]

    ps_ss = nc.alloc_psum_tensor("ps_ss", [1, 512], F32).ap()
    ps_rb = nc.alloc_psum_tensor("ps_rb", [128, 512], F32).ap()
    ps_o = [nc.alloc_psum_tensor(f"ps_oA3{i}", [128, 512], F32).ap() for i in range(2)]

    p.dma("sync", sb_onesb, onesb, sem="cst")
    p.dma("sync", sb_ones1f, ones1f, sem="cst")
    ld0 = p.dma("sync", nw, nrmw, sem="cst")
    p.constap(1e-5)

    g_free = [("tensor", 0)] * 64
    st_free = [[("scalar", 0), ("scalar", 0)], [("vector", 0), ("vector", 0)],
               [("tensor", 0), ("tensor", 0)]]
    f_ss = ("vector", 0)
    f_rb = ("vector", 0)
    castv = 0
    wfree = [("tensor", 0), ("tensor", 0)]
    f_po = [("vector", 0), ("vector", 0)]
    res_free = [("vector", 0), ("vector", 0)]
    x1_prev = 0
    for tt in range(4):
        col0 = tt * 512
        for dk in range(64):
            sl = dk % 2
            ly = p.dma("sync", ystage[sl], yin[dk * 128:(dk + 1) * 128, col0:col0 + 512],
                       waits=[st_free[0][sl]], sem=f"ya{sl}")
            lz = p.dma("sync", zstage[sl], szin[dk * 128:(dk + 1) * 128, col0:col0 + 512],
                       waits=[st_free[1][sl]], sem=f"za{sl}")
            gv = p.op("vector", _tt(gated[dk], ystage[sl], zstage[sl], ALU.mult),
                      waits=[(f"ya{sl}", ly), (f"za{sl}", lz), ("cst", ld0), g_free[dk]])
            st_free[0][sl] = ("vector", gv)
            st_free[1][sl] = ("vector", gv)
            sv = p.op("scalar", _act(sqst[sl], gated[dk], AF.Square),
                      waits=[("vector", gv), st_free[2][sl]])
            t = p.op("tensor", _mm(ps_ss, sb_onesb, sqst[sl], dk == 0, dk == 63),
                     waits=[("scalar", sv), f_ss] if dk == 0 else [("scalar", sv)])
            st_free[2][sl] = ("tensor", t)
        c5a, c5aw = p.constap(1e-5)
        sv = p.op("scalar", _act(rrow, ps_ss, AF.Sqrt, bias=c5a[0:1, :], scale=1.0 / 8192.0),
                  waits=[("tensor", t), ("vector", p.val("vector")), c5aw])
        f_ss = ("scalar", sv)
        rv = p.op("vector", lambda e, a=rrow: e.reciprocal(a, a), waits=[("scalar", sv)])
        t = p.op("tensor", _mm(ps_rb, sb_ones1f, rrow, True, True),
                 waits=[("vector", rv), f_rb, ("cst", 1)])
        rbv = p.op("vector", _cp(rstdbc, ps_rb), waits=[("tensor", t)])
        f_rb = ("vector", rbv)
        gv2 = 0
        for dk in range(64):
            sl = dk % 2
            gva = p.op("vector", _stt(ystage[sl], gated[dk], nw[:, dk:dk + 1], rstdbc,
                                      ALU.mult, ALU.mult),
                       waits=[("vector", rbv), st_free[0][sl]])
            gv2 = p.op("vector", _cp(gated[dk], ystage[sl]), waits=[("vector", gva)])
            st_free[0][sl] = ("vector", gv2)
        ready = []
        for oc in range(32):
            castv = emit_wload(p, wout.tensor, 4096, oc * 128, 128, 64, wst, wbf[oc % 2],
                               wfree[oc % 2], castv)
            pslot = oc % 2
            for dk in range(64):
                w = []
                if dk == 0:
                    w = [("vector", max(castv, gv2)), f_po[pslot]]
                t = p.op("tensor", _mm(ps_o[pslot], wbf[oc % 2][:, dk * 128:(dk + 1) * 128],
                                       gated[dk], dk == 0, dk == 63), waits=w)
            wfree[oc % 2] = ("tensor", t)
            rsl = oc % 2
            lr = p.dma("sync", resst[rsl], xfm[oc * 128:(oc + 1) * 128, col0:col0 + 512],
                       waits=[res_free[rsl]], sem=f"rs{rsl}")
            rv2 = p.op("vector", _tt(outT[oc], ps_o[pslot], resst[rsl], ALU.add),
                       waits=[("tensor", t), (f"rs{rsl}", lr), ("x1", x1_prev)])
            res_free[rsl] = ("vector", rv2)
            f_po[pslot] = ("vector", rv2)
            ready.append(("vector", rv2))
        for dk in range(64):
            g_free[dk] = ("tensor", t)

        def wfn(oc, wait, col0=col0):
            dv = p.dma("gpsimd", x1o[oc * 128:(oc + 1) * 128, col0:col0 + 512],
                       outT[oc], waits=[wait], sem="x1")
            return dv
        emit_ln(p, ln, outT, 32, ready, wfn)
        x1_prev = p.val("x1")
    p.op("gpsimd", lambda e: e.nop(), waits=[("x1", p.val("x1"))])
    p.finalize()
    return nc


def run_a3(y_fm, sz_fm, x_fm, W_out, norm_weight):
    nc = build_a3()
    in_maps = []
    for c in range(NCORE):
        cols = slice(c * 2048, (c + 1) * 2048)
        in_maps.append({
            "yin": np.ascontiguousarray(y_fm[:, cols]),
            "szin": np.ascontiguousarray(sz_fm[:, cols]),
            "xfm": np.ascontiguousarray(x_fm[:, cols]),
            "wout": W_out,
            "nrmw": np.ascontiguousarray(norm_weight.reshape(64, 128).T),
        })
    res = run_bass_kernel_spmd(nc, in_maps, core_ids=list(range(NCORE)))
    return np.concatenate([res.results[c]["x1o"] for c in range(NCORE)], axis=1)


def emit_xtt(p, xdram, c0, xstage, xbf, xst_free, prev_cast, nk=32, sem="xl", cast=True):
    cv = prev_cast
    for k in range(nk):
        sl = k % 2
        lv = p.dma("sync", xstage[sl], xdram[k * 128:(k + 1) * 128, c0:c0 + 512],
                   waits=[xst_free[sl]], sem=f"{sem}{sl}")
        cv = p.op("vector", _cp(xbf[k], xstage[sl]), waits=[(f"{sem}{sl}", lv)])
        xst_free[sl] = ("vector", cv)
    return cv


def build_b1():
    nc = bass.Bass("TRN2", target_bir_lowering=False)
    x1 = nc.dram_tensor("x1", [4096, 2048], F32, kind="ExternalInput").ap()
    rwT = nc.dram_tensor("rwT", [4096, 8], F32, kind="ExternalInput").ap()
    aT = nc.dram_tensor("aT", [4096, 128], F32, kind="ExternalInput").ap()
    bwT = nc.dram_tensor("bwT", [128, 4096], F32, kind="ExternalInput").ap()
    x2o = nc.dram_tensor("x2o", [4096, 2048], F32, kind="ExternalOutput").ap()

    e8_np = np.zeros((8, 128), dtype=np.float32)
    for e in range(8):
        e8_np[e, e * 16:(e + 1) * 16] = 1.0
    e8t = nc.inline_tensor(e8_np, "e8").ap()
    identf = nc.inline_tensor(np.eye(128, dtype=np.float32), "identfB").ap()

    p = Prog(nc)
    A = nc.alloc_sbuf_tensor
    ln = LNctx(nc, p, "b1")
    p.constap(1e-5)
    sb_e8 = A("sb_e8", [8, 128], F32).ap()
    sb_id = A("sb_idB", [128, 128], F32).ap()
    xstage = [A(f"xsgB{i}", [128, 512], F32).ap() for i in range(2)]
    xbf = [A(f"xbfB{k}", [128, 512], BF16).ap() for k in range(32)]
    resst = [A(f"rssB{i}", [128, 512], F32).ap() for i in range(2)]
    rw_f = A("rw_f", [128, 8 * 32], F32).ap()
    rw_b = A("rw_b", [128, 8 * 32], BF16).ap()
    a_f = A("a_f", [128, 4096], F32).ap()
    a_b = A("a_b", [128, 4096], BF16).ap()
    bw_f = A("bw_f", [128, 4096], F32).ap()
    bw_b = A("bw_b", [128, 4096], BF16).ap()
    lg_fm = A("lg_fm", [8, 512], F32).ap()
    lg_tm = A("lg_tm", [128, 8], F32).ap()
    lg_tmp = A("lg_tmp", [128, 8], F32).ap()
    rmax = A("rmaxB", [128, 1], F32).ap()
    gate_fm = A("gate_fm", [8, 512], F32).ap()
    gbc = A("gbcB", [128, 512], BF16).ap()
    wh = A("whB", [128, 512], BF16).ap()
    outT = [A(f"oTB{m}", [128, 512], F32).ap() for m in range(32)]
    ps_lgh = nc.alloc_psum_tensor("ps_lghB", [128, 512], F32).ap()
    ps_tg = nc.alloc_psum_tensor("ps_tgB", [128, 512], F32).ap()
    ps_lg = ps_lgh[0:8, :]
    ps_t = ps_tg[:, 0:128]
    ps_g = ps_tg
    ps_h = ps_lgh
    ps_o = [nc.alloc_psum_tensor(f"ps_oB{i}", [128, 512], F32).ap() for i in range(2)]

    p.dma("sync", sb_e8, e8t, sem="cst")
    p.dma("sync", sb_id, identf, sem="cst")
    rw_ap = bass.AP(rwT.tensor, 0, [[8, 128], [128 * 8, 32], [1, 8]])
    p.dma("sync", rw_f, rw_ap, sem="cst")
    a_ap = bass.AP(aT.tensor, 0, [[128, 128], [128 * 128, 32], [1, 128]])
    p.dma("sync", a_f, a_ap, sem="cst")
    lw = p.dma("sync", bw_f, bwT, sem="cst")
    cv = p.op("vector", _cp(rw_b, rw_f), waits=[("cst", lw)])
    cv = p.op("vector", _cp(a_b, a_f), waits=[("cst", lw)])
    cv = p.op("vector", _cp(bw_b, bw_f), waits=[("cst", lw)])
    WCASTS = cv

    xst_free = [("vector", 0), ("vector", 0)]
    res_free = [("vector", 0), ("vector", 0)]
    f_lg = ("vector", 0)
    f_t = ("vector", 0)
    f_g = ("vector", 0)
    f_h = ("vector", 0)
    wh_rd = ("tensor", 0)
    f_po = [("vector", 0), ("vector", 0)]
    x2_prev = 0
    xcast = WCASTS
    for tt in range(4):
        c0 = tt * 512
        xcast = emit_xtt(p, x1, c0, xstage, xbf, xst_free, xcast)
        t = 0
        for k in range(32):
            w = [("vector", xcast), f_lg] if k == 0 else []
            t = p.op("tensor", _mm(ps_lg, rw_b[:, k * 8:(k + 1) * 8],
                                   xbf[k], k == 0, k == 31), waits=w)
        v = p.op("vector", _cp(lg_fm, ps_lg), waits=[("tensor", t), f_lg])
        f_lg = ("vector", v)
        gfv = 0
        for ts in range(4):
            t = p.op("tensor", _tr(ps_t[:, 0:8], lg_fm[:, ts * 128:(ts + 1) * 128],
                                   sb_id[0:8, 0:8]), waits=[("vector", v), f_t])
            v1 = p.op("vector", _cp(lg_tm, ps_t[:, 0:8]), waits=[("tensor", t)])
            v1 = p.op("vector", _red(rmax, lg_tm, ALU.max), waits=[("vector", v1)])
            v1 = p.op("vector", _ts(lg_tmp, lg_tm, rmax, ALU.is_ge), waits=[("vector", v1)])
            v1 = p.op("vector", _tt(lg_tmp, lg_tmp, lg_tm, ALU.mult), waits=[("vector", v1)])
            t = p.op("tensor", _tr(ps_t[0:8, :], lg_tmp[:, 0:8], sb_id),
                     waits=[("vector", v1)])
            gfv = p.op("vector", _cp(gate_fm[0:8, ts * 128:(ts + 1) * 128], ps_t[0:8, 0:128]),
                       waits=[("tensor", t)])
            f_t = ("vector", gfv)
        t = p.op("tensor", _mm(ps_g, sb_e8, gate_fm, True, True),
                 waits=[("vector", gfv), f_g])
        gbv = p.op("vector", _cp(gbc, ps_g), waits=[("tensor", t), f_g])
        f_g = ("vector", gbv)
        f_t = ("vector", gbv)
        for k in range(32):
            w = [f_h, ("vector", v)] if k == 0 else []
            t = p.op("tensor", _mm(ps_h, a_b[:, k * 128:(k + 1) * 128],
                                   xbf[k], k == 0, k == 31), waits=w)
        whv = p.op("vector", _tt(wh, ps_h, gbc, ALU.mult),
                   waits=[("tensor", t), ("vector", gbv), f_h, wh_rd])
        f_h = ("vector", whv)
        ready = []
        for oc in range(32):
            pslot = oc % 2
            t = p.op("tensor", _mm(ps_o[pslot], bw_b[:, oc * 128:(oc + 1) * 128], wh,
                                   True, True),
                     waits=[("vector", whv), f_po[pslot]])
            rsl = oc % 2
            lr = p.dma("sync", resst[rsl], x1[oc * 128:(oc + 1) * 128, c0:c0 + 512],
                       waits=[res_free[rsl]], sem=f"rs{rsl}")
            rv2 = p.op("vector", _tt(outT[oc], ps_o[pslot], resst[rsl], ALU.add),
                       waits=[("tensor", t), (f"rs{rsl}", lr), ("x2", x2_prev)])
            res_free[rsl] = ("vector", rv2)
            f_po[pslot] = ("vector", rv2)
            ready.append(("vector", rv2))

        wh_rd = ("tensor", t)

        def wfn(oc, wait, c0=c0):
            return p.dma("gpsimd", x2o[oc * 128:(oc + 1) * 128, c0:c0 + 512],
                         outT[oc], waits=[wait], sem="x2")
        emit_ln(p, ln, outT, 32, ready, wfn)
        x2_prev = p.val("x2")
    p.op("gpsimd", lambda e: e.nop(), waits=[("x2", p.val("x2"))])
    p.finalize()
    return nc


def run_b1(x1_fm, route_w, A_, Bw):
    nc = build_b1()
    aT = np.ascontiguousarray(A_.reshape(8 * 16, 4096).T)
    bwT = np.ascontiguousarray(Bw.transpose(0, 2, 1).reshape(128, 4096))
    rwT = np.ascontiguousarray(route_w.T)
    in_maps = []
    for c in range(NCORE):
        cols = slice(c * 2048, (c + 1) * 2048)
        in_maps.append({"x1": np.ascontiguousarray(x1_fm[:, cols]),
                        "rwT": rwT, "aT": aT, "bwT": bwT})
    res = run_bass_kernel_spmd(nc, in_maps, core_ids=list(range(NCORE)))
    return np.concatenate([res.results[c]["x2o"] for c in range(NCORE)], axis=1)


def build_c1():
    nc = bass.Bass("TRN2", target_bir_lowering=False)
    x2 = nc.dram_tensor("x2", [4096, 2048], F32, kind="ExternalInput").ap()
    wqT = nc.dram_tensor("wqT", [4096, 6144], F32, kind="ExternalInput").ap()
    wgT = nc.dram_tensor("wgT", [4096, 96], F32, kind="ExternalInput").ap()
    bg = nc.dram_tensor("bg", [96, 1], F32, kind="ExternalInput").ap()
    qkvo = nc.dram_tensor("qkvo", [6144, 2048], BF16, kind="ExternalOutput").ap()
    glino = nc.dram_tensor("glino", [96, 2048], F32, kind="ExternalOutput").ap()

    p = Prog(nc)
    A = nc.alloc_sbuf_tensor
    xst = A("xstC", [128, 2048], F32).ap()
    xbf = [A(f"xbfC{k}", [128, 2048], BF16).ap() for k in range(32)]
    sb_bg = A("sb_bg", [96, 1], F32).ap()
    wst = A("wstC", [128, 2048], F32).ap()
    wbf = [A(f"wbfC{i}", [128, 4096], BF16).ap() for i in range(2)]
    ost = [A(f"ostC{i}", [128, 512], BF16).ap() for i in range(2)]
    gst = [A(f"gstC{i}", [96, 512], F32).ap() for i in range(2)]
    ps_o = [nc.alloc_psum_tensor(f"ps_oC{i}", [128, 512], F32).ap() for i in range(2)]
    ps_g = nc.alloc_psum_tensor("ps_gC", [96, 512], F32).ap()

    p.dma("sync", sb_bg, bg, sem="cst")
    cv = 0
    for k in range(32):
        lxv = p.dma("sync", xst, x2[k * 128:(k + 1) * 128, :],
                    waits=[("vector", cv)], sem="xl")
        cv = p.op("vector", _cp(xbf[k], xst), waits=[("xl", lxv)])
    CASTS = cv

    castv = 0
    wfree = [("tensor", 0), ("tensor", 0)]
    f_po = [("vector", 0), ("vector", 0)]
    f_pg = ("scalar", 0)
    odma = []
    t = 0
    for mc in range(48):
        castv = emit_wload(p, wqT.tensor, 6144, mc * 128, 128, 32, wst, wbf[mc % 2],
                           wfree[mc % 2], castv)
        for tt in range(4):
            pslot = tt % 2
            for k in range(32):
                w = []
                if k == 0:
                    w = [("vector", max(castv, CASTS)), f_po[pslot]]
                t = p.op("tensor", _mm(ps_o[pslot], wbf[mc % 2][:, k * 128:(k + 1) * 128],
                                       xbf[k][:, tt * 512:(tt + 1) * 512], k == 0, k == 31),
                         waits=w)
            slot = len(odma) % 2
            w2 = [("tensor", t)]
            if len(odma) >= 2:
                w2.append(odma[-2])
            ev = p.op("vector", _cp(ost[slot], ps_o[pslot]), waits=w2)
            f_po[pslot] = ("vector", ev)
            dv = p.dma("gpsimd", qkvo[mc * 128:(mc + 1) * 128, tt * 512:(tt + 1) * 512],
                       ost[slot], waits=[("vector", ev)], sem=f"qo{slot}")
            odma.append((f"qo{slot}", dv))
        wfree[mc % 2] = ("tensor", t)
    castv = emit_wload(p, wgT.tensor, 96, 0, 96, 32, wst, wbf[0], wfree[0], castv)
    gdma = []
    for tt in range(4):
        for k in range(32):
            w = []
            if k == 0:
                w = [("vector", castv), f_pg]
            t = p.op("tensor", _mm(ps_g, wbf[0][:, k * 96:(k + 1) * 96],
                                   xbf[k][:, tt * 512:(tt + 1) * 512], k == 0, k == 31),
                     waits=w)
        slot = tt % 2
        w2 = [("tensor", t), ("cst", 1)]
        if tt >= 2:
            w2.append(gdma[tt - 2])
        ev = p.op("scalar", _act(gst[slot], ps_g, AF.Identity, bias=sb_bg), waits=w2)
        f_pg = ("scalar", ev)
        dv = p.dma("gpsimd", glino[:, tt * 512:(tt + 1) * 512], gst[slot],
                   waits=[("scalar", ev)], sem=f"go{slot}")
        gdma.append((f"go{slot}", dv))
    p.op("gpsimd", lambda e: e.nop(),
         waits=[(s, p.val(s)) for s in ("qo0", "qo1", "go0", "go1")])
    p.finalize()
    return nc


def run_c1(x2_fm, w_qkv, w_gate, b_gate):
    nc = build_c1()
    wqT = np.ascontiguousarray(w_qkv.T)
    wgT = np.ascontiguousarray(w_gate.T)
    in_maps = []
    for c in range(NCORE):
        cols = slice(c * 2048, (c + 1) * 2048)
        in_maps.append({"x2": np.ascontiguousarray(x2_fm[:, cols]),
                        "wqT": wqT, "wgT": wgT,
                        "bg": np.ascontiguousarray(b_gate[:, None])})
    res = run_bass_kernel_spmd(nc, in_maps, core_ids=list(range(NCORE)))
    qkvT = np.concatenate([res.results[c]["qkvo"] for c in range(NCORE)], axis=1)
    glinT = np.concatenate([res.results[c]["glino"] for c in range(NCORE)], axis=1)
    return qkvT, glinT


def build_c3():
    nc = bass.Bass("TRN2", target_bir_lowering=False)
    att = nc.dram_tensor("att", [4096, 2048], BF16, kind="ExternalInput").ap()
    x2f = nc.dram_tensor("x2f", [4096, 2048], F32, kind="ExternalInput").ap()
    woT = nc.dram_tensor("woT", [4096, 4096], F32, kind="ExternalInput").ap()
    x3o = nc.dram_tensor("x3o", [4096, 2048], F32, kind="ExternalOutput").ap()
    p = Prog(nc)
    A = nc.alloc_sbuf_tensor
    ln = LNctx(nc, p, "c3")
    p.constap(1e-5)
    abf = [A(f"abfC3{k}", [128, 512], BF16).ap() for k in range(32)]
    resst = [A(f"rssC3{i}", [128, 512], F32).ap() for i in range(2)]
    wst = A("wstC3", [128, 2048], F32).ap()
    wbf = [A(f"wbfC3{i}", [128, 4096], BF16).ap() for i in range(2)]
    outT = [A(f"oTC3{m}", [128, 512], F32).ap() for m in range(32)]
    ps_o = [nc.alloc_psum_tensor(f"ps_oC3{i}", [128, 512], F32).ap() for i in range(2)]

    castv = 0
    wfree = [("tensor", 0), ("tensor", 0)]
    f_po = [("vector", 0), ("vector", 0)]
    res_free = [("vector", 0), ("vector", 0)]
    abf_free = [("tensor", 0)] * 32
    x3_prev = 0
    t = 0
    for tt in range(4):
        c0 = tt * 512
        alast = 0
        for k in range(32):
            alast = p.dma("sync", abf[k], att[k * 128:(k + 1) * 128, c0:c0 + 512],
                          waits=[abf_free[k]], sem=f"ab{k % 2}")
        ready = []
        for oc in range(32):
            castv = emit_wload(p, woT.tensor, 4096, oc * 128, 128, 32, wst, wbf[oc % 2],
                               wfree[oc % 2], castv)
            pslot = oc % 2
            for k in range(32):
                w = []
                if k == 0:
                    w = [("vector", castv), (f"ab0", p.val("ab0")), (f"ab1", p.val("ab1")),
                         f_po[pslot]]
                t = p.op("tensor", _mm(ps_o[pslot], wbf[oc % 2][:, k * 128:(k + 1) * 128],
                                       abf[k], k == 0, k == 31), waits=w)
            wfree[oc % 2] = ("tensor", t)
            rsl = oc % 2
            lr = p.dma("sync", resst[rsl], x2f[oc * 128:(oc + 1) * 128, c0:c0 + 512],
                       waits=[res_free[rsl]], sem=f"rs{rsl}")
            rv2 = p.op("vector", _tt(outT[oc], ps_o[pslot], resst[rsl], ALU.add),
                       waits=[("tensor", t), (f"rs{rsl}", lr), ("x3", x3_prev)])
            res_free[rsl] = ("vector", rv2)
            f_po[pslot] = ("vector", rv2)
            ready.append(("vector", rv2))
        for k in range(32):
            abf_free[k] = ("tensor", t)

        def wfn(oc, wait, c0=c0):
            return p.dma("gpsimd", x3o[oc * 128:(oc + 1) * 128, c0:c0 + 512],
                         outT[oc], waits=[wait], sem="x3")
        emit_ln(p, ln, outT, 32, ready, wfn)
        x3_prev = p.val("x3")
    p.op("gpsimd", lambda e: e.nop(), waits=[("x3", p.val("x3"))])
    p.finalize()
    return nc


def run_c3(att_fm, x2_fm, w_o):
    nc = build_c3()
    woT = np.ascontiguousarray(w_o.T)
    in_maps = []
    for c in range(NCORE):
        cols = slice(c * 2048, (c + 1) * 2048)
        in_maps.append({"att": np.ascontiguousarray(att_fm[:, cols]),
                        "x2f": np.ascontiguousarray(x2_fm[:, cols]), "woT": woT})
    res = run_bass_kernel_spmd(nc, in_maps, core_ids=list(range(NCORE)))
    return np.concatenate([res.results[c]["x3o"] for c in range(NCORE)], axis=1)


def build_d1():
    nc = bass.Bass("TRN2", target_bir_lowering=False)
    x3 = nc.dram_tensor("x3", [4096, 2048], F32, kind="ExternalInput").ap()
    wT = nc.dram_tensor("wT", [4096, 4096], F32, kind="ExternalInput").ap()
    aT = nc.dram_tensor("aT", [4096, 256], F32, kind="ExternalInput").ap()
    bfT = nc.dram_tensor("bfT", [256, 4096], F32, kind="ExternalInput").ap()
    biasw = nc.dram_tensor("biasw", [128, 32], F32, kind="ExternalInput").ap()
    coef = nc.dram_tensor("coef", [16, 1], F32, kind="ExternalInput").ap()
    scal = nc.dram_tensor("scal", [16, 1], F32, kind="ExternalInput").ap()
    x4o = nc.dram_tensor("x4o", [4096, 2048], F32, kind="ExternalOutput").ap()

    er_np = np.zeros((16, 2 * 128), dtype=np.float32)
    for ch in range(2):
        for m in range(128):
            er_np[ch * 8 + m // 16, ch * 128 + m] = 1.0
    ert = nc.inline_tensor(er_np, "erD").ap()

    p = Prog(nc)
    A = nc.alloc_sbuf_tensor
    ln = LNctx(nc, p, "d1")
    p.constap(1e-5)
    xstage = [A(f"xsgD{i}", [128, 512], F32).ap() for i in range(2)]
    xbf = [A(f"xbfD{k}", [128, 512], BF16).ap() for k in range(32)]
    resst = [A(f"rssD{i}", [128, 512], F32).ap() for i in range(2)]
    a_b = A("a_bD", [128, 8192], BF16).ap()
    bf_b = [A(f"bf_bD{i}", [128, 4096], BF16).ap() for i in range(2)]
    sb_bias = A("sb_biasD", [128, 32], F32).ap()
    sb_co = A("sb_co", [16, 1], F32).ap()
    sb_sc = A("sb_sc", [16, 1], F32).ap()
    sb_w16 = A("sb_w16", [16, 1], F32).ap()
    sb_er = A("sb_er", [16, 256], F32).ap()
    wc = [A(f"wcD{i}", [128, 1], F32).ap() for i in range(2)]
    xa_sc = [A(f"xa_scD{i}", [128, 512], BF16).ap() for i in range(2)]
    wst = A("wstD", [128, 2048], F32).ap()
    wbf = [A(f"wbfD{i}", [128, 4096], BF16).ap() for i in range(2)]
    outT = [A(f"oTD{m}", [128, 512], F32).ap() for m in range(32)]
    tres = A("tresD", [128, 512], F32).ap()
    ps_o = [nc.alloc_psum_tensor(f"ps_oD{i}", [128, 512], F32).ap() for i in range(2)]
    ps_xa = [nc.alloc_psum_tensor(f"ps_xaD{i}", [128, 512], F32).ap() for i in range(2)]
    ps_m = nc.alloc_psum_tensor("ps_mD", [128, 1], F32).ap()

    p.dma("sync", sb_bias, biasw, sem="cst")
    p.dma("sync", sb_co, coef, sem="cst")
    p.dma("sync", sb_sc, scal, sem="cst")
    lv = p.dma("sync", sb_er, ert, sem="cst")
    castv = 0
    for part in range(4):
        a_ap = bass.AP(aT.tensor, part * 8 * 128 * 256,
                       [[256, 128], [128 * 256, 8], [1, 256]])
        lw = p.dma("sync", wst, a_ap, waits=[("vector", castv)], sem="w")
        castv = p.op("vector", _cp(a_b[:, part * 2048:(part + 1) * 2048], wst),
                     waits=[("w", lw)])
    for ch in range(2):
        for half in range(2):
            bf_ap = bass.AP(bfT.tensor, ch * 128 * 4096 + half * 2048,
                            [[4096, 128], [1, 2048]])
            lw = p.dma("sync", wst, bf_ap, waits=[("vector", castv)], sem="w")
            castv = p.op("vector", _cp(bf_b[ch][:, half * 2048:(half + 1) * 2048], wst),
                         waits=[("w", lw)])
    cvB = p.op("vector", _ts(sb_w16, sb_co, sb_sc, ALU.mult), waits=[("cst", lv)])
    tw = 0
    cv = castv
    for ch in range(2):
        tw = p.op("tensor", _mm(ps_m, sb_er[:, ch * 128:(ch + 1) * 128], sb_w16,
                                True, True),
                  waits=[("vector", cvB), ("vector", cv) if ch > 0 else ("vector", 0)])
        cv = p.op("vector", _cp(wc[ch], ps_m), waits=[("tensor", tw)])
    WCV = cv

    xst_free = [("vector", 0), ("vector", 0)]
    res_free = [("vector", 0), ("vector", 0)]
    wfree = [("tensor", 0), ("tensor", 0)]
    f_po = [("vector", 0), ("vector", 0)]
    f_xa = [("vector", 0), ("vector", 0)]
    xa_rd = ("tensor", 0)
    f_tres = ("vector", 0)
    x4_prev = 0
    xcast = WCV
    t = 0
    for tt in range(4):
        c0 = tt * 512
        xcast = emit_xtt(p, x3, c0, xstage, xbf, xst_free, xcast)
        for ch in range(2):
            for k in range(32):
                w = []
                if k == 0:
                    w = [("vector", xcast), f_xa[ch]]
                t = p.op("tensor", _mm(ps_xa[ch],
                                       a_b[:, k * 256 + ch * 128:k * 256 + (ch + 1) * 128],
                                       xbf[k], k == 0, k == 31), waits=w)
            v = p.op("vector", _ts(xa_sc[ch], ps_xa[ch], wc[ch], ALU.mult),
                     waits=[("tensor", t), f_xa[ch], xa_rd])
            f_xa[ch] = ("vector", v)
        XAV = f_xa[1][1]
        ready = []
        for oc in range(32):
            castv = emit_wload(p, wT.tensor, 4096, oc * 128, 128, 32, wst, wbf[oc % 2],
                               wfree[oc % 2], castv)
            pslot = oc % 2
            for k in range(32):
                w = []
                if k == 0:
                    w = [("vector", max(castv, XAV)), f_po[pslot]]
                t = p.op("tensor", _mm(ps_o[pslot], wbf[oc % 2][:, k * 128:(k + 1) * 128],
                                       xbf[k], k == 0, False), waits=w)
            for ch in range(2):
                t = p.op("tensor", _mm(ps_o[pslot], bf_b[ch][:, oc * 128:(oc + 1) * 128],
                                       xa_sc[ch], False, ch == 1))
            wfree[oc % 2] = ("tensor", t)
            rsl = oc % 2
            lr = p.dma("sync", resst[rsl], x3[oc * 128:(oc + 1) * 128, c0:c0 + 512],
                       waits=[res_free[rsl]], sem=f"rs{rsl}")
            v = p.op("vector", _ts(tres, ps_o[pslot], sb_bias[:, oc:oc + 1], ALU.add),
                     waits=[("tensor", t), ("cst", 1), f_tres])
            rv2 = p.op("vector", _tt(outT[oc], tres, resst[rsl], ALU.add),
                       waits=[("vector", v), (f"rs{rsl}", lr), ("x4", x4_prev)])
            f_tres = ("vector", rv2)
            res_free[rsl] = ("vector", rv2)
            f_po[pslot] = ("vector", rv2)
            ready.append(("vector", rv2))

        xa_rd = ("tensor", t)

        def wfn(oc, wait, c0=c0):
            return p.dma("gpsimd", x4o[oc * 128:(oc + 1) * 128, c0:c0 + 512],
                         outT[oc], waits=[wait], sem="x4")
        emit_ln(p, ln, outT, 32, ready, wfn)
        x4_prev = p.val("x4")
    p.op("gpsimd", lambda e: e.nop(), waits=[("x4", p.val("x4"))])
    p.finalize()
    return nc


def run_d1(x3_fm, W, bias, A_, Bf, coeffs, scales):
    nc = build_d1()
    wT = np.ascontiguousarray(W.T)
    aT = np.ascontiguousarray(A_.reshape(256, 4096).T)
    bfT = np.ascontiguousarray(Bf.transpose(0, 2, 1).reshape(256, 4096))
    in_maps = []
    for c in range(NCORE):
        cols = slice(c * 2048, (c + 1) * 2048)
        in_maps.append({"x3": np.ascontiguousarray(x3_fm[:, cols]),
                        "wT": wT, "aT": aT, "bfT": bfT,
                        "biasw": np.ascontiguousarray(bias.reshape(32, 128).T),
                        "coef": np.ascontiguousarray(coeffs[:, None]),
                        "scal": np.ascontiguousarray(scales[:, None])})
    res = run_bass_kernel_spmd(nc, in_maps, core_ids=list(range(NCORE)))
    return np.concatenate([res.results[c]["x4o"] for c in range(NCORE)], axis=1)


NEGBIG_F = float(np.float32(-3.0e38))

SC_SHIFT = -15.0
NSA_SCALE = float(128.0 ** -0.5)


def build_c2():
    nc = bass.Bass("TRN2", target_bir_lowering=False)
    ins = {}
    for ui in range(2):
        for nm, shp, dt in (("qo", [512, 8192], BF16), ("qp", [512, 8192], BF16),
                            ("ko", [128, 8192], BF16), ("kp", [128, 8192], BF16),
                            ("vo", [128, 8192], BF16), ("gl", [8192, 12], F32),
                            ("kpos", [128, 16], F32), ("vpos", [128, 16], F32),
                            ("memk", [128, 1], F32), ("memv", [1, 128], F32)):
            ins[f"{nm}{ui}"] = nc.dram_tensor(f"{nm}{ui}", shp, dt, kind="ExternalInput").ap()
    kw1T = nc.dram_tensor("kw1T", [2048, 2048], F32, kind="ExternalInput").ap()
    kw2T = nc.dram_tensor("kw2T", [2048, 128], F32, kind="ExternalInput").ap()
    kb1 = nc.dram_tensor("kb1", [128, 16], F32, kind="ExternalInput").ap()
    kb2 = nc.dram_tensor("kb2", [128, 1], F32, kind="ExternalInput").ap()
    vw1T = nc.dram_tensor("vw1T", [2048, 2048], F32, kind="ExternalInput").ap()
    vw2T = nc.dram_tensor("vw2T", [2048, 128], F32, kind="ExternalInput").ap()
    vb1 = nc.dram_tensor("vb1", [128, 16], F32, kind="ExternalInput").ap()
    vb2 = nc.dram_tensor("vb2", [1, 128], F32, kind="ExternalInput").ap()
    atto = nc.dram_tensor("atto", [1024, 8192], BF16, kind="ExternalOutput").ap()

    identf = nc.inline_tensor(np.eye(128, dtype=np.float32), "idC2f").ap()
    identb = nc.inline_tensor(np.eye(128, dtype=BFNP), "idC2b").ap()
    inv = 1.0 / 10000.0 ** (np.arange(0, 128, 2, dtype=np.float64) / 128.0)
    ang = np.arange(8192, dtype=np.float64)[None, :] * inv[:, None]
    cos_np = np.cos(ang).astype(BFNP)
    sin_np = np.sin(ang).astype(BFNP)
    cosT = nc.inline_tensor(np.concatenate([cos_np, cos_np], 0), "cosT").ap()
    sinT = nc.inline_tensor(np.concatenate([sin_np, sin_np], 0), "sinT").ap()
    jj, tl = np.meshgrid(np.arange(128), np.arange(512), indexing="ij")
    cm_np = [np.where(tl >= 16 * jj + 16 - o, 0.0, -3e38).astype(BFNP)
             for o in (0, 512, 1024, 1536)]
    cmT = nc.inline_tensor(np.concatenate(cm_np, axis=1), "cmT").ap()
    own_np = [(((jj // 16 + 8 * m) == (tl // 16)) & (jj % 16 <= tl % 16)).astype(BFNP)
              for m in range(4)]
    ownT = nc.inline_tensor(np.concatenate(own_np, axis=1), "ownT").ap()
    jb, tb = np.meshgrid(np.arange(128), np.arange(64), indexing="ij")
    bandT = nc.inline_tensor(((jb >= tb) & (jb <= tb + 64)).astype(BFNP), "bandT").ap()
    bandAT = nc.inline_tensor((jb >= tb + 64).astype(BFNP), "bandAT").ap()
    band0T = nc.inline_tensor((jb[0:64] <= tb[0:64]).astype(BFNP), "band0T").ap()
    es_np = np.zeros((128, 16 * 128), dtype=BFNP)
    for vvar in range(16):
        for j in range(128):
            es_np[vvar * 8 + j // 16, vvar * 128 + j] = 1.0
    esT = nc.inline_tensor(es_np, "esT").ap()
    e12_np = np.zeros((12, 12 * 128), dtype=BFNP)
    for r in range(12):
        e12_np[r, r * 128:(r + 1) * 128] = 1.0
    e12T = nc.inline_tensor(e12_np, "e12T").ap()
    onesbT = nc.inline_tensor(np.ones((128, 1), dtype=BFNP), "onesbC").ap()
    ones1fT = nc.inline_tensor(np.ones((1, 128), dtype=np.float32), "ones1C").ap()

    p = Prog(nc)
    A = nc.alloc_sbuf_tensor
    c15, _ = p.constap(SC_SHIFT)
    sb_idf = A("sb_idfC", [128, 128], F32).ap()
    sb_idb = A("sb_idbC", [128, 128], BF16).ap()
    sb_cos = A("sb_cos", [128, 8192], BF16).ap()
    sb_sin = A("sb_sin", [128, 8192], BF16).ap()
    sb_cm = A("sb_cm", [128, 2048], BF16).ap()
    sb_own = A("sb_own", [128, 2048], BF16).ap()
    sb_band = A("sb_band", [128, 64], BF16).ap()
    sb_bandA = A("sb_bandA", [128, 64], BF16).ap()
    sb_band0 = A("sb_band0", [64, 64], BF16).ap()
    sb_es = A("sb_es", [128, 2048], BF16).ap()
    sb_e12 = A("sb_e12", [12, 1536], BF16).ap()
    sb_onesb = A("sb_onesbC", [128, 1], BF16).ap()
    sb_ones1f = A("sb_ones1fC", [1, 128], F32).ap()
    tmp2 = [A(f"tmp2C{i}", [128, 512], BF16).ap() for i in range(2)]
    KTP = A("KTPC", [128, 8192], BF16).ap()
    V16 = A("V16C", [128, 8192], BF16).ap()
    RKT = A("RKTC", [128, 8192], BF16).ap()
    VTM = A("VTMC", [128, 8192], BF16).ap()
    h1 = [A(f"h1C{i}", [128, 512], BF16).ap() for i in range(16)]
    ckT = A("ckTC", [128, 512], BF16).ap()
    cv_tm = [A(f"cvtmC{i}", [128, 128], BF16).ap() for i in range(4)]
    w2res = A("w2resC", [128, 2048], BF16).ap()
    wstg = [A(f"wstgC{i}", [128, 128], F32).ap() for i in range(2)]
    wbfg = [A(f"wbfgC{i}", [128, 128], BF16).ap() for i in range(2)]
    sb_pos = A("sb_pos", [128, 16], F32).ap()
    sb_memk = A("sb_memkC", [128, 1], BF16).ap()
    sb_memv = A("sb_memvC", [1, 128], BF16).ap()
    sb_vb2 = A("sb_vb2C", [1, 128], F32).ap()
    sb_b2bc = A("sb_b2bc", [128, 128], F32).ap()
    sb_b1 = A("sb_b1C", [128, 16], F32).ap()
    sb_b2 = A("sb_b2C", [128, 1], F32).ap()
    qo_sl = A("qoslC", [128, 512], BF16).ap()
    rq_sl = [A(f"rqslC{g}", [128, 512], BF16).ap() for g in range(4)]
    qp_b = A("qpslC", [128, 512], BF16).ap()
    rt1 = A("rt1C", [64, 512], BF16).ap()
    rt2 = A("rt2C", [64, 512], BF16).ap()
    impacc = [A(f"impC{i}", [128, 512], F32).ap() for i in range(4)]
    sc_f = [A(f"scfC{i}", [128, 512], F32).ap() for i in range(2)]
    pt_b = [A(f"ptbC{i}", [128, 512], BF16).ap() for i in range(2)]
    pt_b2 = A("ptb2C", [64, 64], BF16).ap()
    selT = [A(f"selTC{i}", [128, 512], BF16).ap() for i in range(4)]
    imp_tm = A("imptmC", [128, 512], F32).ap()
    wk = [A(f"wkC{i}", [128, 512], F32).ap() for i in range(2)]
    tmpm = A("tmpmC", [128, 512], F32).ap()
    rmaxc = A("rmaxcC", [128, 1], F32).ap()
    ducol = A("ducolC", [128, 1], F32).ap()
    thrcol = A("thrcolC", [128, 1], F32).ap()
    etile = A("etileC", [128, 512], BF16).ap()
    seltm = A("seltmC", [128, 512], BF16).ap()
    coutT = [A(f"coutC{g}", [128, 512], BF16).ap() for g in range(4)]
    foutT = [A(f"foutC{g}", [128, 512], BF16).ap() for g in range(4)]
    soutT = [A(f"soutC{g}", [128, 512], BF16).ap() for g in range(4)]
    kmsb = A("kmsbC", [128, 512], BF16).ap()
    denrow = A("denrowC", [1, 512], F32).ap()
    rdall = [A(f"rdallC{i}", [128, 12], F32).ap() for i in range(4)]
    g_tm = A("g_tmC", [128, 12], F32).ap()
    grT = A("grTC", [12, 512], BF16).ap()
    gbc_sb = [A(f"gbcC{i}", [128, 512], F32).ap() for i in range(3)]
    t_cmb = [A(f"tcmbC{i}", [128, 512], F32).ap() for i in range(2)]
    comb_b = [A(f"combC{i}", [128, 512], BF16).ap() for i in range(2)]

    ps_sc = nc.alloc_psum_tensor("ps_scC", [128, 512], F32).ap()
    ps_km = nc.alloc_psum_tensor("ps_kmC", [128, 512], F32).ap()
    ps_av = nc.alloc_psum_tensor("ps_avC", [128, 512], F32).ap()
    ps_den = nc.alloc_psum_tensor("ps_denC", [1, 512], F32).ap()
    ps_mem = nc.alloc_psum_tensor("ps_memC", [1, 512], F32).ap()
    ps_tr = nc.alloc_psum_tensor("ps_trC", [128, 128], F32).ap()
    ps_trb = nc.alloc_psum_tensor("ps_trbC", [128, 128], BF16).ap()

    CST = 0
    for (dst, srct) in ((sb_idf, identf), (sb_idb, identb), (sb_cos, cosT),
                        (sb_sin, sinT), (sb_cm, cmT), (sb_own, ownT),
                        (sb_band, bandT), (sb_bandA, bandAT), (sb_band0, band0T), (sb_es, esT),
                        (sb_e12, e12T), (sb_onesb, onesbT), (sb_ones1f, ones1fT)):
        CST = p.dma("sync", dst, srct, sem="cst")

    fr = {
        "TMP": ("vector", 0), "KTP": ("tensor", 0), "V16": ("tensor", 0),
        "RKT": ("tensor", 0), "VTM": ("tensor", 0), "ckT": ("tensor", 0),
        "w2res": ("tensor", 0), "b1": ("scalar", 0), "b2": ("scalar", 0),
        "pos": ("vector", 0), "memk": ("tensor", 0), "memv": ("tensor", 0),
        "vb2": ("tensor", 0), "b2bc": ("vector", 0), "qo": ("tensor", 0),
        "qp": ("vector", 0),
        "ps_sc": ("vector", 0), "ps_km": ("vector", 0), "ps_av": ("vector", 0),
        "ps_den": ("vector", 0), "ps_mem": ("scalar", 0), "ps_tr": ("vector", 0),
        "ps_trb": ("vector", 0),
        "sc": [("scalar", 0), ("scalar", 0)], "pt": [("tensor", 0), ("tensor", 0)],
        "imp": [("tensor", 0)] * 4, "selT": [("tensor", 0)] * 4,
        "cout": [("vector", 0)] * 4, "fout": [("vector", 0)] * 4,
        "sout": [("vector", 0)] * 4, "rq": [("tensor", 0)] * 4,
        "kmsb": ("vector", 0), "denrow": ("tensor", 0), "pt2": ("tensor", 0),
        "rdall": [("tensor", 0)] * 4, "gtm": ("vector", 0), "grT": ("tensor", 0),
        "gbc": [("vector", 0)] * 3, "tcmb": [("vector", 0), ("vector", 0)],
        "comb": [0, 0], "h1": [("tensor", 0)] * 16, "cvtm": [("tensor", 0)] * 4,
        "etile": ("vector", 0), "imptm": ("vector", 0), "wk": [("vector", 0)] * 2,
        "tmpm": ("vector", 0), "seltm": ("tensor", 0),
    }
    tmp_rd = [0, 0]
    wslot_cast = [0, 0]
    wslot_mm = [("tensor", 0), ("tensor", 0)]
    ncmb = 0

    def wload(srcten, off, row_len):
        s = wload.n % 2
        wload.n += 1
        lw = p.dma("sync", wstg[s], bass.AP(srcten, off, [[row_len, 128], [1, 128]]),
                   waits=[("vector", wslot_cast[s])], sem=f"wg{s}")
        cv = p.op("vector", _cp(wbfg[s], wstg[s]), waits=[(f"wg{s}", lw), wslot_mm[s]])
        wslot_cast[s] = cv
        return s, cv
    wload.n = 0

    for ui in range(2):
        rv = 0
        lkp = [0, 0]
        for s16 in range(16):
            sl = slice(s16 * 512, (s16 + 1) * 512)
            ts2 = s16 % 2
            wl = [fr["TMP"]] if s16 < 2 else [("vector", tmp_rd[ts2])]
            lkp[ts2] = p.dma("sync", tmp2[ts2], ins[f"kp{ui}"][:, sl],
                             waits=wl, sem=f"tmp{ts2}")
            w0 = [(f"tmp{ts2}", lkp[ts2]), ("cst", CST)]
            if s16 == 0:
                w0.append(fr["RKT"])
            v1 = p.op("vector", _tt(rt1, tmp2[ts2][0:64, :], sb_cos[0:64, sl], ALU.mult), waits=w0)
            v1 = p.op("vector", _tt(rt2, tmp2[ts2][64:128, :], sb_sin[64:128, sl], ALU.mult),
                      waits=[("vector", v1)])
            v1 = p.op("vector", _tt(RKT[0:64, sl], rt1, rt2, ALU.subtract),
                      waits=[("vector", v1)])
            v1 = p.op("vector", _tt(rt1, tmp2[ts2][64:128, :], sb_cos[64:128, sl], ALU.mult),
                      waits=[("vector", v1)])
            v1 = p.op("vector", _tt(rt2, tmp2[ts2][0:64, :], sb_sin[0:64, sl], ALU.mult),
                      waits=[("vector", v1)])
            rv = p.op("vector", _tt(RKT[64:128, sl], rt1, rt2, ALU.add),
                      waits=[("vector", v1)])
            tmp_rd[ts2] = rv
        RKT_RDY = ("vector", rv)

        lb1 = p.dma("sync", sb_b1, kb1, waits=[fr["b1"]], sem="aux")
        lb2 = p.dma("sync", sb_b2, kb2, waits=[fr["b2"]], sem="aux")
        lpos = p.dma("sync", sb_pos, ins[f"kpos{ui}"], waits=[fr["pos"]], sem="aux")
        ktv = 0
        for blk in range(16):
            ts2 = blk % 2
            lko = p.dma("sync", tmp2[ts2], ins[f"ko{ui}"][:, blk * 512:(blk + 1) * 512],
                        waits=[("vector", tmp_rd[ts2])], sem=f"tmp{ts2}")
            for i in range(16):
                w0 = [(f"tmp{ts2}", lko)]
                if blk == 0 and i == 0:
                    w0 += [("aux", lpos), fr["KTP"]]
                ktv = p.op("vector", _ts(KTP[:, blk * 512 + i:(blk + 1) * 512:16],
                                         tmp2[ts2][:, i::16], sb_pos[:, i:i + 1],
                                         ALU.add), waits=w0)
            tmp_rd[ts2] = ktv
        KTP_W = ("vector", ktv)
        t = 0
        for cc in range(16):
            for i in range(16):
                s, cv = wload(kw1T.tensor, (i * 128) * 2048 + cc * 128, 2048)
                w0 = [("vector", max(cv, ktv))]
                if i == 0:
                    w0 += [fr["ps_sc"], fr["h1"][cc]]
                t = p.op("tensor", _mm(ps_sc, wbfg[s], KTP[:, i::16], i == 0, i == 15),
                         waits=w0)
                wslot_mm[s] = ("tensor", t)
            sv = p.op("scalar", _act(h1[cc], ps_sc, AF.Relu, bias=sb_b1[:, cc:cc + 1]),
                      waits=[("tensor", t), ("aux", lb1)])
            fr["ps_sc"] = ("scalar", sv)
            fr["h1"][cc] = ("scalar", sv)
        w2v = 0
        for cc in range(16):
            s, cv = wload(kw2T.tensor, (cc * 128) * 128, 128)
            w2v = p.op("vector", _cp(w2res[:, cc * 128:(cc + 1) * 128], wbfg[s]),
                       waits=[("vector", cv), fr["w2res"] if cc == 0 else ("vector", 0)])
        t = 0
        for cc in range(16):
            w0 = [("vector", w2v), fr["h1"][cc]]
            if cc == 0:
                w0.append(fr["ps_km"])
            t = p.op("tensor", _mm(ps_km, w2res[:, cc * 128:(cc + 1) * 128], h1[cc],
                                   cc == 0, cc == 15), waits=w0)
        CKMM = t
        sv = p.op("scalar", _act(ckT, ps_km, AF.Identity, bias=sb_b2),
                  waits=[("tensor", t), ("aux", lb2), fr["ckT"]])
        fr["ps_km"] = ("scalar", sv)
        CKT_RDY = ("scalar", sv)

        lvo = p.dma("sync", V16, ins[f"vo{ui}"], waits=[fr["V16"]], sem="v16")
        lb1 = p.dma("sync", sb_b1, vb1, waits=[fr["ps_sc"]], sem="aux")
        lvb2 = p.dma("sync", sb_vb2, vb2, waits=[fr["vb2"]], sem="aux")
        lpos = p.dma("sync", sb_pos, ins[f"vpos{ui}"], waits=[KTP_W], sem="aux")
        ktv = 0
        for i in range(16):
            w0 = [("v16", lvo), ("aux", lpos), ("tensor", CKMM)] if i == 0 else []
            ktv = p.op("vector", _ts(KTP[:, i::16], V16[:, i::16], sb_pos[:, i:i + 1],
                                     ALU.add), waits=w0)
        for cc in range(16):
            for i in range(16):
                s, cv = wload(vw1T.tensor, (i * 128) * 2048 + cc * 128, 2048)
                w0 = [("vector", max(cv, ktv))]
                if i == 0:
                    w0 += [fr["ps_sc"], fr["h1"][cc], CKT_RDY]
                t = p.op("tensor", _mm(ps_sc, wbfg[s], KTP[:, i::16], i == 0, i == 15),
                         waits=w0)
                wslot_mm[s] = ("tensor", t)
            sv = p.op("scalar", _act(h1[cc], ps_sc, AF.Relu, bias=sb_b1[:, cc:cc + 1]),
                      waits=[("tensor", t), ("aux", lb1)])
            fr["ps_sc"] = ("scalar", sv)
            fr["h1"][cc] = ("scalar", sv)
        w2v = 0
        for cc in range(16):
            s, cv = wload(vw2T.tensor, (cc * 128) * 128, 128)
            w2v = p.op("vector", _cp(w2res[:, cc * 128:(cc + 1) * 128], wbfg[s]),
                       waits=[("vector", cv), ("tensor", CKMM) if cc == 0 else ("vector", 0)])
        tb = p.op("tensor", _mm(ps_tr, sb_ones1f, sb_vb2, True, True),
                  waits=[("aux", lvb2), fr["ps_tr"], ("cst", CST)])
        bv = p.op("vector", _cp(sb_b2bc, ps_tr), waits=[("tensor", tb), fr["b2bc"]])
        fr["ps_tr"] = ("vector", bv)
        for jc in range(4):
            t = 0
            for cc in range(16):
                w0 = [("vector", w2v), fr["h1"][cc]]
                if cc == 0:
                    w0 += [fr["ps_sc"], fr["cvtm"][jc]]
                t = p.op("tensor", _mm(ps_sc[:, 0:128], h1[cc][:, jc * 128:(jc + 1) * 128],
                                       w2res[:, cc * 128:(cc + 1) * 128],
                                       cc == 0, cc == 15), waits=w0)
            vv = p.op("vector", _tt(cv_tm[jc], ps_sc[:, 0:128], sb_b2bc, ALU.add),
                      waits=[("tensor", t), ("vector", bv)])
            fr["ps_sc"] = ("vector", vv)
            fr["cvtm"][jc] = ("vector", vv)
        CV_RDY = ("vector", vv)

        trv = 0
        for cb in range(64):
            tt_ = p.op("tensor", _tr(ps_trb, V16[:, cb * 128:(cb + 1) * 128], sb_idb),
                       waits=[("v16", lvo), fr["ps_trb"], fr["VTM"]] if cb == 0
                       else [fr["ps_trb"]])
            trv = p.op("vector", _cp(VTM[:, cb * 128:(cb + 1) * 128], ps_trb),
                       waits=[("tensor", tt_)])
            fr["ps_trb"] = ("vector", trv)
        VTM_RDY = ("vector", trv)
        lmk = p.dma("sync", sb_pos[:, 0:1], ins[f"memk{ui}"], waits=[fr["memk"]], sem="aux")
        mkv = p.op("vector", _cp(sb_memk, sb_pos[:, 0:1]), waits=[("aux", lmk), fr["memk"]])
        lmv = p.dma("sync", sb_vb2, ins[f"memv{ui}"], waits=[("tensor", tb)], sem="aux")
        mvv = p.op("vector", _cp(sb_memv, sb_vb2), waits=[("aux", lmv), fr["memv"]])
        MEM_RDY = ("vector", max(mkv, mvv))

        last_mm = 0
        for tt in range(16):
            tt0 = tt * 512
            jcs_c = [jc for jc in range(4) if 2048 * jc <= 512 * tt + 495]
            for g in range(4):
                lqo = p.dma("sync", qo_sl, ins[f"qo{ui}"][g * 128:(g + 1) * 128, tt0:tt0 + 512],
                            waits=[fr["qo"]], sem="qo")
                lqp = p.dma("sync", qp_b, ins[f"qp{ui}"][g * 128:(g + 1) * 128, tt0:tt0 + 512],
                            waits=[fr["qp"]], sem="qp")
                w0 = [("qp", lqp), RKT_RDY, fr["rq"][g]]
                v1 = p.op("vector", _tt(rt1, qp_b[0:64, :], sb_cos[0:64, tt0:tt0 + 512], ALU.mult), waits=w0)
                v1 = p.op("vector", _tt(rt2, qp_b[64:128, :], sb_sin[64:128, tt0:tt0 + 512], ALU.mult),
                          waits=[("vector", v1)])
                v1 = p.op("vector", _tt(rq_sl[g][0:64, :], rt1, rt2, ALU.subtract),
                          waits=[("vector", v1)])
                v1 = p.op("vector", _tt(rt1, qp_b[64:128, :], sb_cos[64:128, tt0:tt0 + 512], ALU.mult),
                          waits=[("vector", v1)])
                v1 = p.op("vector", _tt(rt2, qp_b[0:64, :], sb_sin[0:64, tt0:tt0 + 512], ALU.mult),
                          waits=[("vector", v1)])
                vq = p.op("vector", _tt(rq_sl[g][64:128, :], rt1, rt2, ALU.add),
                          waits=[("vector", v1)])
                fr["qp"] = ("vector", vq)
                RQ_RDY = vq
                tm_ = p.op("tensor", _mm(ps_mem, sb_memk, qo_sl, True, True),
                           waits=[("qo", lqo), MEM_RDY, CKT_RDY, fr["ps_mem"]])
                fr["qo"] = ("tensor", tm_)
                nav = 0
                for ji, jc in enumerate(jcs_c):
                    t = p.op("tensor", _mm(ps_sc, ckT[:, jc * 128:(jc + 1) * 128], qo_sl,
                                           True, True), waits=[fr["ps_sc"]])
                    sslot = ji % 2
                    o = 512 * tt - 2048 * jc
                    if o >= 2032:
                        vv = p.op("vector", _ts(sc_f[sslot], ps_sc, NSA_SCALE, ALU.mult),
                                  waits=[("tensor", t), fr["sc"][sslot]])
                    else:
                        vv = p.op("vector", _stt(sc_f[sslot], ps_sc, NSA_SCALE,
                                                 sb_cm[:, (o // 512) * 512:(o // 512) * 512 + 512],
                                                 ALU.mult, ALU.add),
                                  waits=[("tensor", t), fr["sc"][sslot]])
                    fr["ps_sc"] = ("vector", vv)
                    if g == 0:
                        iv = p.op("vector", _cp(impacc[jc], sc_f[sslot]),
                                  waits=[("vector", vv), fr["imp"][jc]])
                    else:
                        iv = p.op("vector", _tt(impacc[jc], impacc[jc], sc_f[sslot], ALU.add),
                                  waits=[("vector", vv)])
                    sv = p.op("scalar", _act(pt_b[sslot], sc_f[sslot], AF.Exp, bias=c15),
                              waits=[("vector", iv), fr["pt"][sslot]])
                    fr["sc"][sslot] = ("scalar", sv)
                    t = p.op("tensor", _mm(ps_av, cv_tm[jc], pt_b[sslot], ji == 0, False),
                             waits=[("scalar", sv), CV_RDY] + ([fr["ps_av"]] if ji == 0 else []))
                    t = p.op("tensor", _mm(ps_den, sb_onesb, pt_b[sslot], ji == 0, False),
                             waits=[fr["ps_den"]] if ji == 0 else [])
                    fr["pt"][sslot] = ("tensor", t)
                    nav += 1
                smv = p.op("scalar", _act(pt_b[0][0:1, :], ps_mem, AF.Exp, bias=c15[0:1, :],
                                          scale=NSA_SCALE),
                           waits=[("tensor", tm_), fr["pt"][0]])
                fr["ps_mem"] = ("scalar", smv)
                t = p.op("tensor", _mm(ps_av, sb_memv, pt_b[0][0:1, :], nav == 0, True),
                         waits=[("scalar", smv)] + ([fr["ps_av"]] if nav == 0 else []))
                t2 = p.op("tensor", _mm(ps_den, sb_onesb[0:1, :], pt_b[0][0:1, :],
                                        nav == 0, True),
                          waits=[fr["ps_den"]] if nav == 0 else [])
                fr["pt"][0] = ("tensor", max(t, t2))
                cov = p.op("vector", _cp(coutT[g], ps_av), waits=[("tensor", t), fr["cout"][g]])
                fr["ps_av"] = ("vector", cov)
                fr["cout"][g] = ("vector", cov)
                dv = p.op("vector", lambda e, a=denrow, b=ps_den: e.reciprocal(a, b),
                          waits=[("tensor", t2), fr["denrow"]])
                fr["ps_den"] = ("vector", dv)
                for ts in range(4):
                    tt_ = p.op("tensor", _tr(ps_tr[:, 0:1], denrow[0:1, ts * 128:(ts + 1) * 128],
                                             sb_idf[0:1, 0:1]),
                               waits=[("vector", dv), fr["ps_tr"]])
                    rdv = p.op("vector", _cp(rdall[ts][:, 3 * g:3 * g + 1], ps_tr[:, 0:1]),
                               waits=[("tensor", tt_), fr["rdall"][ts]])
                    fr["ps_tr"] = ("vector", rdv)
                fr["denrow"] = ("tensor", tt_)
                fr["qo"] = ("tensor", p.val("tensor"))
            for jc in range(4):
                if jc not in jcs_c:
                    iv = p.op("vector", _memset(impacc[jc], -3.0e38), waits=[fr["imp"][jc]])
            for ts in range(4):
                tv = 0
                for jc in range(4):
                    tt_ = p.op("tensor", _tr(ps_tr, impacc[jc][:, ts * 128:(ts + 1) * 128],
                                             sb_idf),
                               waits=[("vector", p.val("vector")), fr["ps_tr"]])
                    tv = p.op("vector", _cp(imp_tm[:, jc * 128:(jc + 1) * 128], ps_tr),
                              waits=[("tensor", tt_), fr["imptm"] if jc == 0 else ("vector", 0)])
                    fr["ps_tr"] = ("vector", tv)
                for jc in range(4):
                    fr["imp"][jc] = ("tensor", tt_)
                wv = p.op("vector", _cp(wk[0], imp_tm), waits=[("vector", tv), fr["wk"][0]])
                cur = 0
                for rnd in range(4):
                    rv1 = p.op("vector", _red(rmaxc, wk[cur], ALU.max), waits=[("vector", wv)])
                    if rnd < 3:
                        mv = p.op("vector", _ts(tmpm, wk[cur], rmaxc, ALU.is_ge),
                                  waits=[("vector", rv1), fr["tmpm"]])
                        wv = p.op("vector", _stt(wk[1 - cur], tmpm, NEGBIG_F, wk[cur],
                                                 ALU.mult, ALU.add),
                                  waits=[("vector", mv), fr["wk"][1 - cur]])
                        fr["tmpm"] = ("vector", wv)
                        cur = 1 - cur
                fr["wk"] = [("vector", rv1), ("vector", rv1)]
                ev = p.op("scalar", _act(etile, imp_tm, AF.Exp, bias=c15, scale=0.25),
                          waits=[("vector", tv), fr["etile"]])
                dv = p.op("vector", _red(ducol, etile, ALU.add), waits=[("scalar", ev)])
                fr["etile"] = ("vector", dv)
                sv = p.op("scalar", _act(ducol, ducol, AF.Ln), waits=[("vector", dv)])
                cv2 = p.op("vector", _ts(ducol, ducol, 4.0, ALU.mult, -32.10340371976184,
                                         ALU.add), waits=[("scalar", sv)])
                thv = p.op("vector", _tt(thrcol, rmaxc, ducol, ALU.max),
                           waits=[("vector", max(cv2, rv1))])
                thv = p.op("vector", _ts(thrcol, thrcol, -1.0e37, ALU.max),
                           waits=[("vector", thv)])
                sev = p.op("vector", _ts(seltm, imp_tm, thrcol, ALU.is_ge),
                           waits=[("vector", thv), fr["seltm"]])
                fr["imptm"] = ("vector", sev)
                for jc in range(4):
                    tt_ = p.op("tensor", _tr(ps_trb, seltm[:, jc * 128:(jc + 1) * 128], sb_idb),
                               waits=[("vector", sev), fr["ps_trb"]])
                    vv = p.op("vector", _cp(selT[jc][:, ts * 128:(ts + 1) * 128], ps_trb),
                              waits=[("tensor", tt_), fr["selT"][jc] if ts == 0 else ("vector", 0)])
                    fr["ps_trb"] = ("vector", vv)
                fr["seltm"] = ("tensor", tt_)
                SEL_RDY = vv
            for g in range(4):
                njc2 = tt * 4 + 4
                for jc2 in range(njc2):
                    t = p.op("tensor", _mm(ps_sc, RKT[:, jc2 * 128:(jc2 + 1) * 128], rq_sl[g],
                                           True, True),
                             waits=[fr["ps_sc"]] + ([("vector", RQ_RDY), RKT_RDY] if jc2 == 0 else []))
                    sslot = jc2 % 2
                    sv = p.op("scalar", _act(sc_f[sslot], ps_sc, AF.Exp, bias=c15,
                                             scale=NSA_SCALE),
                              waits=[("tensor", t), fr["sc"][sslot]])
                    fr["ps_sc"] = ("scalar", sv)
                    tk = p.op("tensor", _mm(ps_km, sb_es[:, (jc2 % 16) * 128:(jc2 % 16 + 1) * 128],
                                            selT[jc2 // 16], True, True),
                              waits=[fr["ps_km"]] + ([("vector", SEL_RDY)] if jc2 == 0 else []))
                    if jc2 >= tt * 4:
                        m = jc2 - tt * 4
                        kv = p.op("vector", _tt(kmsb, ps_km, sb_own[:, m * 512:(m + 1) * 512],
                                                ALU.add),
                                  waits=[("tensor", tk), fr["kmsb"]])
                        pv = p.op("vector", _tt(pt_b[sslot], sc_f[sslot], kmsb, ALU.mult),
                                  waits=[("vector", kv), ("scalar", sv), fr["pt"][sslot]])
                        fr["kmsb"] = ("vector", pv)
                    else:
                        pv = p.op("vector", _tt(pt_b[sslot], sc_f[sslot], ps_km, ALU.mult),
                                  waits=[("tensor", tk), ("scalar", sv), fr["pt"][sslot]])
                    fr["ps_km"] = ("vector", pv)
                    fr["sc"][sslot] = ("vector", pv)
                    t = p.op("tensor", _mm(ps_av, VTM[:, jc2 * 128:(jc2 + 1) * 128],
                                           pt_b[sslot], jc2 == 0, jc2 == njc2 - 1),
                             waits=[("vector", pv), VTM_RDY] + ([fr["ps_av"]] if jc2 == 0 else []))
                    t2 = p.op("tensor", _mm(ps_den, sb_onesb, pt_b[sslot],
                                            jc2 == 0, jc2 == njc2 - 1),
                              waits=[fr["ps_den"]] if jc2 == 0 else [])
                    fr["pt"][sslot] = ("tensor", max(t, t2))
                fov = p.op("vector", _cp(foutT[g], ps_av), waits=[("tensor", t), fr["fout"][g]])
                fr["ps_av"] = ("vector", fov)
                fr["fout"][g] = ("vector", fov)
                dv = p.op("vector", lambda e, a=denrow, b=ps_den: e.reciprocal(a, b),
                          waits=[("tensor", t2), fr["denrow"]])
                fr["ps_den"] = ("vector", dv)
                for ts in range(4):
                    tt_ = p.op("tensor", _tr(ps_tr[:, 0:1], denrow[0:1, ts * 128:(ts + 1) * 128],
                                             sb_idf[0:1, 0:1]),
                               waits=[("vector", dv), fr["ps_tr"]])
                    rdv = p.op("vector", _cp(rdall[ts][:, 3 * g + 1:3 * g + 2], ps_tr[:, 0:1]),
                               waits=[("tensor", tt_)])
                    fr["ps_tr"] = ("vector", rdv)
                fr["denrow"] = ("tensor", tt_)
                nw0 = tt * 8
                for wi in range(8):
                    w_ = nw0 + wi
                    tl0 = wi * 64
                    if w_ == 0:
                        t = p.op("tensor", _mm(ps_sc[0:64, 0:64], RKT[:, 0:64],
                                               rq_sl[g][:, 0:64], True, True),
                                 waits=[fr["ps_sc"]])
                        sv = p.op("scalar", _act(sc_f[0][0:64, 0:64], ps_sc[0:64, 0:64],
                                                 AF.Exp, bias=c15[0:64, :], scale=NSA_SCALE),
                                  waits=[("tensor", t), fr["sc"][0]])
                        pv = p.op("vector", _tt(pt_b[0][0:64, 0:64], sc_f[0][0:64, 0:64],
                                                sb_band0, ALU.mult),
                                  waits=[("scalar", sv), fr["pt"][0]])
                        fr["ps_sc"] = ("vector", pv)
                        fr["sc"][0] = ("vector", pv)
                        t = p.op("tensor", _mm(ps_av[:, 0:64], VTM[0:64, 0:128],
                                               pt_b[0][0:64, 0:64], True, True),
                                 waits=[("vector", pv), fr["ps_av"]])
                        t2 = p.op("tensor", _mm(ps_den[:, 0:64], sb_onesb[0:64, :],
                                                pt_b[0][0:64, 0:64], True, True),
                                  waits=[fr["ps_den"]])
                        fr["pt"][0] = ("tensor", max(t, t2))
                    else:
                        sslot = wi % 2
                        if w_ % 2 == 0:
                            ca = w_ // 2
                            t = p.op("tensor", _mm(ps_sc[:, 0:64],
                                                   RKT[:, (ca - 1) * 128:ca * 128],
                                                   rq_sl[g][:, tl0:tl0 + 64], True, True),
                                     waits=[fr["ps_sc"]] + ([fr["ps_av"], fr["ps_den"]] if wi == 0 else []))
                            tB = p.op("tensor", _mm(ps_sc[0:64, 64:128],
                                                    RKT[:, ca * 128:ca * 128 + 64],
                                                    rq_sl[g][:, tl0:tl0 + 64], True, True))
                            sv = p.op("scalar", _act(sc_f[sslot][:, 0:64], ps_sc[:, 0:64],
                                                     AF.Exp, bias=c15, scale=NSA_SCALE),
                                      waits=[("tensor", tB), fr["sc"][sslot]])
                            sv = p.op("scalar", _act(sc_f[sslot][0:64, 64:128],
                                                     ps_sc[0:64, 64:128],
                                                     AF.Exp, bias=c15[0:64, :], scale=NSA_SCALE),
                                      waits=[("scalar", sv)])
                            pv = p.op("vector", _tt(pt_b[sslot][:, 0:64], sc_f[sslot][:, 0:64],
                                                    sb_bandA, ALU.mult),
                                      waits=[("scalar", sv), fr["pt"][sslot]])
                            pv = p.op("vector", _tt(pt_b2[0:64, 0:64],
                                                    sc_f[sslot][0:64, 64:128], sb_band0,
                                                    ALU.mult),
                                      waits=[("vector", pv), fr["pt2"]])
                            fr["ps_sc"] = ("vector", pv)
                            fr["sc"][sslot] = ("vector", pv)
                        else:
                            t = p.op("tensor", _mm(ps_sc[:, 0:64], RKT[:, w_ * 64 - 64:w_ * 64 + 64],
                                                   rq_sl[g][:, tl0:tl0 + 64], True, True),
                                     waits=[fr["ps_sc"]] + ([fr["ps_av"], fr["ps_den"]] if wi == 0 else []))
                            sv = p.op("scalar", _act(sc_f[sslot][:, 0:64], ps_sc[:, 0:64],
                                                     AF.Exp, bias=c15, scale=NSA_SCALE),
                                      waits=[("tensor", t), fr["sc"][sslot]])
                            pv = p.op("vector", _tt(pt_b[sslot][:, 0:64], sc_f[sslot][:, 0:64],
                                                    sb_band, ALU.mult),
                                      waits=[("scalar", sv), fr["pt"][sslot]])
                            fr["ps_sc"] = ("vector", pv)
                            fr["sc"][sslot] = ("vector", pv)
                        if w_ % 2 == 0:
                            ca = w_ // 2
                            t = p.op("tensor", _mm(ps_av[:, tl0:tl0 + 64],
                                                   VTM[:, (ca - 1) * 128:ca * 128],
                                                   pt_b[sslot][:, 0:64], True, False),
                                     waits=[("vector", pv)])
                            t = p.op("tensor", _mm(ps_av[:, tl0:tl0 + 64],
                                                   VTM[0:64, ca * 128:ca * 128 + 128],
                                                   pt_b2[0:64, 0:64], False, True))
                            t2 = p.op("tensor", _mm(ps_den[:, tl0:tl0 + 64], sb_onesb,
                                                    pt_b[sslot][:, 0:64], True, False))
                            t2 = p.op("tensor", _mm(ps_den[:, tl0:tl0 + 64], sb_onesb[0:64, :],
                                                    pt_b2[0:64, 0:64], False, True))
                            fr["pt2"] = ("tensor", t2)
                        else:
                            ca = (w_ - 1) // 2
                            t = p.op("tensor", _mm(ps_av[:, tl0:tl0 + 64],
                                                   VTM[:, ca * 128:ca * 128 + 128],
                                                   pt_b[sslot][:, 0:64], True, True),
                                     waits=[("vector", pv)])
                            t2 = p.op("tensor", _mm(ps_den[:, tl0:tl0 + 64], sb_onesb,
                                                    pt_b[sslot][:, 0:64], True, True))
                        fr["pt"][sslot] = ("tensor", max(t, t2))
                sov = p.op("vector", _cp(soutT[g], ps_av), waits=[("tensor", t), fr["sout"][g]])
                fr["ps_av"] = ("vector", sov)
                fr["sout"][g] = ("vector", sov)
                dv = p.op("vector", lambda e, a=denrow, b=ps_den: e.reciprocal(a, b),
                          waits=[("tensor", t2), fr["denrow"]])
                fr["ps_den"] = ("vector", dv)
                for ts in range(4):
                    tt_ = p.op("tensor", _tr(ps_tr[:, 0:1], denrow[0:1, ts * 128:(ts + 1) * 128],
                                             sb_idf[0:1, 0:1]),
                               waits=[("vector", dv), fr["ps_tr"]])
                    rdv = p.op("vector", _cp(rdall[ts][:, 3 * g + 2:3 * g + 3], ps_tr[:, 0:1]),
                               waits=[("tensor", tt_)])
                    fr["ps_tr"] = ("vector", rdv)
                fr["denrow"] = ("tensor", tt_)
            for jc in range(4):
                fr["selT"][jc] = ("tensor", p.val("tensor"))
            for g in range(4):
                fr["rq"][g] = ("tensor", p.val("tensor"))
            gv = 0
            for ts in range(4):
                lg = p.dma("sync", g_tm, ins[f"gl{ui}"][tt0 + ts * 128:tt0 + ts * 128 + 128, :],
                           waits=[fr["gtm"]], sem="gl")
                sv = p.op("scalar", _act(g_tm, g_tm, AF.Sigmoid), waits=[("gl", lg)])
                vv = p.op("vector", _tt(g_tm, g_tm, rdall[ts], ALU.mult),
                          waits=[("scalar", sv), ("vector", rdv)])
                tt_ = p.op("tensor", _tr(ps_tr[0:12, :], g_tm[:, 0:12], sb_idf),
                           waits=[("vector", vv), fr["ps_tr"]])
                fr["gtm"] = ("tensor", tt_)
                gv = p.op("vector", _cp(grT[:, ts * 128:(ts + 1) * 128], ps_tr[0:12, 0:128]),
                          waits=[("tensor", tt_), fr["grT"] if ts == 0 else ("vector", 0)])
                fr["ps_tr"] = ("vector", gv)
            for g in range(4):
                for br in range(3):
                    tt_ = p.op("tensor", _mm(ps_km, sb_e12[:, (3 * g + br) * 128:(3 * g + br + 1) * 128],
                                             grT, True, True),
                               waits=[("vector", gv), fr["ps_km"]])
                    vv = p.op("vector", _cp(gbc_sb[br], ps_km),
                              waits=[("tensor", tt_), fr["gbc"][br]])
                    fr["ps_km"] = ("vector", vv)
                fr["grT"] = ("tensor", tt_)
                cs = ncmb % 2
                v1 = p.op("vector", _tt(t_cmb[0], coutT[g], gbc_sb[0], ALU.mult),
                          waits=[("vector", vv), fr["cout"][g], fr["tcmb"][0]])
                v2 = p.op("vector", _tt(t_cmb[1], foutT[g], gbc_sb[1], ALU.mult),
                          waits=[("vector", v1), fr["tcmb"][1]])
                v3 = p.op("vector", _tt(t_cmb[0], t_cmb[0], t_cmb[1], ALU.add),
                          waits=[("vector", v2)])
                v4 = p.op("vector", _tt(t_cmb[1], soutT[g], gbc_sb[2], ALU.mult),
                          waits=[("vector", v3)])
                w5 = [("vector", v4)]
                if fr["comb"][cs]:
                    w5.append((f"ao{cs}", fr["comb"][cs]))
                v5 = p.op("vector", _tt(comb_b[cs], t_cmb[0], t_cmb[1], ALU.add), waits=w5)
                fr["tcmb"] = [("vector", v5), ("vector", v5)]
                fr["gbc"] = [("vector", v5)] * 3
                fr["cout"][g] = ("vector", v5)
                fr["fout"][g] = ("vector", v5)
                fr["sout"][g] = ("vector", v5)
                dvv = p.dma("gpsimd", atto[ui * 512 + g * 128:ui * 512 + (g + 1) * 128,
                                          tt0:tt0 + 512],
                            comb_b[cs], waits=[("vector", v5)], sem=f"ao{cs}")
                fr["comb"][cs] = dvv
                ncmb += 1
                for ts in range(4):
                    fr["rdall"][ts] = ("tensor", tt_)
        fr["TMP"] = ("vector", ktv)
        tmp_rd = [p.val("vector")] * 2
        fr["KTP"] = ("tensor", t)
        fr["V16"] = ("tensor", t)
        fr["RKT"] = ("tensor", t)
        fr["VTM"] = ("tensor", t)
        fr["ckT"] = ("tensor", t)
        fr["w2res"] = ("tensor", t)
        fr["b1"] = fr["ps_sc"]
        fr["b2"] = CKT_RDY
        fr["pos"] = ("vector", p.val("vector"))
        fr["memk"] = ("tensor", t)
        fr["memv"] = ("tensor", t)
        fr["vb2"] = ("tensor", t)
        fr["b2bc"] = ("vector", p.val("vector"))
        for g in range(4):
            fr["rq"][g] = ("tensor", t)
        for jc in range(4):
            fr["selT"][jc] = ("tensor", t)
            fr["cvtm"][jc] = ("tensor", t)
        for cc in range(16):
            fr["h1"][cc] = ("tensor", t)
    p.op("gpsimd", lambda e: e.nop(),
         waits=[(s, p.val(s)) for s in ("ao0", "ao1")])
    p.finalize()
    return nc


def run_c2(qkvT, glin_tm, kw1, kw2, kb1_, kb2_, vw1, vw2, vb1_, vb2_,
           k_pos, v_pos, mem_kv):
    import ml_dtypes
    nc = build_c2()
    perm = np.concatenate([np.arange(0, 128, 2), np.arange(1, 128, 2)])
    kw1T = np.ascontiguousarray(kw1.T)
    kw2T = np.ascontiguousarray(kw2.T)
    vw1T = np.ascontiguousarray(vw1.T)
    vw2T = np.ascontiguousarray(vw2.T)
    in_maps = []
    for c in range(NCORE):
        m = {}
        for ui in range(2):
            u = 2 * c + ui
            b, kvh = u // 8, u % 8
            cols = slice(b * 8192, (b + 1) * 8192)
            qrows = qkvT[kvh * 512:(kvh + 1) * 512, cols]
            krow = qkvT[4096 + kvh * 128:4096 + (kvh + 1) * 128, cols]
            vrow = qkvT[5120 + kvh * 128:5120 + (kvh + 1) * 128, cols]
            qp = qrows.reshape(4, 128, 8192)[:, perm, :].reshape(512, 8192)
            m[f"qo{ui}"] = np.ascontiguousarray(qrows)
            m[f"qp{ui}"] = np.ascontiguousarray(qp)
            m[f"ko{ui}"] = np.ascontiguousarray(krow)
            m[f"kp{ui}"] = np.ascontiguousarray(krow[perm, :])
            m[f"vo{ui}"] = np.ascontiguousarray(vrow)
            m[f"gl{ui}"] = np.ascontiguousarray(
                glin_tm[cols, kvh * 12:(kvh + 1) * 12].astype(np.float32))
            m[f"kpos{ui}"] = np.ascontiguousarray(k_pos[kvh].T)
            m[f"vpos{ui}"] = np.ascontiguousarray(v_pos[kvh].T)
            m[f"memk{ui}"] = np.ascontiguousarray(mem_kv[0, kvh].reshape(128, 1))
            m[f"memv{ui}"] = np.ascontiguousarray(mem_kv[1, kvh].reshape(1, 128))
        m.update({"kw1T": kw1T, "kw2T": kw2T, "kb1": np.ascontiguousarray(kb1_.reshape(16, 128).T),
                  "kb2": np.ascontiguousarray(kb2_.reshape(128, 1)),
                  "vw1T": vw1T, "vw2T": vw2T, "vb1": np.ascontiguousarray(vb1_.reshape(16, 128).T),
                  "vb2": np.ascontiguousarray(vb2_.reshape(1, 128))})
        in_maps.append(m)
    res = run_bass_kernel_spmd(nc, in_maps, core_ids=list(range(NCORE)))
    att_fm = np.zeros((4096, 16384), dtype=ml_dtypes.bfloat16)
    for c in range(NCORE):
        for ui in range(2):
            u = 2 * c + ui
            b, kvh = u // 8, u % 8
            att_fm[kvh * 512:(kvh + 1) * 512, b * 8192:(b + 1) * 8192] = \
                res.results[c]["atto"][ui * 512:(ui + 1) * 512]
    return att_fm



def kernel(**inputs):
    x = np.ascontiguousarray(inputs["L0_attn_hidden_states"].reshape(16384, 4096))
    z_s, xbc_s, dt_s = run_a1(x, inputs["L0_attn_W_in"], inputs["L0_attn_conv_w"])
    y_fm = run_a2(xbc_s, dt_s, inputs["L0_attn_dt_bias"], inputs["L0_attn_A_log"],
                  inputs["L0_attn_D_param"])
    x_fm = np.ascontiguousarray(x.T)
    x1_fm = run_a3(y_fm, z_s, x_fm, inputs["L0_attn_W_out"],
                   inputs["L0_attn_norm_weight"])
    x2_fm = run_b1(x1_fm, inputs["L0_ffn_route_w"], inputs["L0_ffn_A"],
                   inputs["L0_ffn_Bw"])
    qkvT, glinT = run_c1(x2_fm, inputs["L1_attn_w_qkv"], inputs["L1_attn_w_gate"],
                         inputs["L1_attn_b_gate"])
    att_fm = run_c2(qkvT, np.ascontiguousarray(glinT.T),
                    inputs["L1_attn_k_w1"], inputs["L1_attn_k_w2"],
                    inputs["L1_attn_k_b1"], inputs["L1_attn_k_b2"],
                    inputs["L1_attn_v_w1"], inputs["L1_attn_v_w2"],
                    inputs["L1_attn_v_b1"], inputs["L1_attn_v_b2"],
                    inputs["L1_attn_k_pos"], inputs["L1_attn_v_pos"],
                    inputs["L1_attn_mem_kv"])
    x3_fm = run_c3(att_fm, x2_fm, inputs["L1_attn_w_o"])
    x4_fm = run_d1(x3_fm, inputs["L1_ffn_W"], inputs["L1_ffn_bias"],
                   inputs["L1_ffn_A"], inputs["L1_ffn_Bf"],
                   inputs["L1_ffn_coeffs"], inputs["L1_ffn_scales"])
    out = np.ascontiguousarray(x4_fm.T).reshape(2, 8192, 4096).astype(np.float32)
    return out
